# version 20
# baseline (speedup 1.0000x reference)
import contextlib
import numpy as np

import concourse.bass as bass
import concourse.mybir as mybir
from concourse.bass_utils import run_bass_kernel_spmd

F32 = mybir.dt.float32
BF16 = mybir.dt.bfloat16
AF = mybir.ActivationFunctionType
ALU = mybir.AluOpType
AX = mybir.AxisListType

MAGIC = float(np.float32(3 * 2**22))
EPS_LN = 1e-5
EPS_ATTN = 1e-6
PAIRS = [[0, 1], [2, 3], [4, 5], [6, 7]]
ALL8 = [list(range(8))]


class Ch:
    def __init__(self, es, nc, name):
        self.sem = es.enter_context(nc.semaphore(name))
        self.n = 0

    def inc(self, instr, k=1):
        instr.then_inc(self.sem, k)
        self.n += k
        return instr

    def wait(self, eng, v):
        if v > 0:
            eng.wait_ge(self.sem, v)


class ChP:

    def __init__(self, es, nc, name, per=16, nring=2):
        self.s = [es.enter_context(nc.semaphore(f"{name}{i}"))
                  for i in range(nring)]
        self.per = per
        self.nr = nring

    def inc(self, instr, item, k=16):
        instr.then_inc(self.s[item % self.nr], k)
        return instr

    def wait_item(self, eng, item):
        if item < 0:
            return
        eng.wait_ge(self.s[item % self.nr], self.per * (item // self.nr + 1))

    def wait_upto(self, eng, item):
        for i in range(self.nr):
            self.wait_item(eng, item - i)


def bcast_ap(handle, off):
    return bass.AP(handle, off, [[0, 128], [1, 1]])


def rows3(ap2d):
    return ap2d.rearrange("(n p) d -> p n d", p=128)


def sb3(ap2d, n):
    return ap2d.rearrange("p (n d) -> p n d", n=n)


def build_nc(LQ=8192, D=1024, DI=2816, debug=False):
    TOK = LQ // 2
    NTT = LQ // 512
    NA = 4 * NTT
    ND = D // 128
    EHW = D // 2
    NEC = EHW // 128
    NIC = DI // 128
    NHC = D // 128
    NG = TOK // 512
    NOB = TOK // 128
    NW = 2 * NIC + NHC
    NQF = NOB // 4
    q4 = LQ // 4
    SPANS = [(0, q4), (q4, 2 * q4), (2 * q4, 3 * q4),
             (3 * q4, 3 * q4 + q4 // 2), (3 * q4 + q4 // 2, 4 * q4)]
    NCH = len(SPANS)
    CIDX = []
    for _t in range(NOB):
        for _ci, (_s, _e) in enumerate(SPANS):
            if _t * 128 < _e // 2:
                CIDX.append(_ci)
                break

    def ps2_row(t0):
        return t0

    nc = bass.Bass("TRN2", target_bir_lowering=False,
                   detect_race_conditions=False)

    xb = nc.dram_tensor("xb", [LQ, D], F32, kind="ExternalInput")
    x_own = nc.dram_tensor("x_own", [TOK, D], F32, kind="ExternalInput")
    wq_d = nc.dram_tensor("wq", [D, EHW], F32, kind="ExternalInput")
    wk_d = nc.dram_tensor("wk", [D, EHW], F32, kind="ExternalInput")
    wv_d = nc.dram_tensor("wv", [D, EHW], F32, kind="ExternalInput")
    wc_d = nc.dram_tensor("wc", [EHW, D], F32, kind="ExternalInput")
    wg_d = nc.dram_tensor("wg", [DI, D], F32, kind="ExternalInput")
    wu_d = nc.dram_tensor("wu", [DI, D], F32, kind="ExternalInput")
    wd_d = nc.dram_tensor("wd", [D, DI], F32, kind="ExternalInput")
    clipx_d = nc.dram_tensor("clipx", [1, 1], F32, kind="ExternalInput")
    clipd_d = nc.dram_tensor("clipd", [1, 1], F32, kind="ExternalInput")
    out_d = nc.dram_tensor("out", [TOK, D], F32, kind="ExternalOutput")

    dk = "ExternalOutput" if debug else "Internal"
    xn_scr = nc.dram_tensor("xn_scr", [LQ, D], BF16, kind=dk)
    ps_scr = nc.dram_tensor("ps_scr", [LQ, D], BF16)
    rs_scr = nc.dram_tensor("rs_scr", [TOK, D], BF16)
    xq_scr = nc.dram_tensor("xq_scr", [TOK, D], BF16, kind=dk)
    wg_scr = nc.dram_tensor("wg_scr", [DI, D], BF16, kind=dk)
    wu_scr = nc.dram_tensor("wu_scr", [DI, D], BF16, kind=dk)
    wd_scr = nc.dram_tensor("wd_scr", [D, DI], BF16, kind=dk)
    hh_scr = nc.dram_tensor("hh_scr", [DI, TOK], BF16, kind=dk)
    y_scr = nc.dram_tensor("y_scr", [TOK, D], BF16, kind=dk)
    mx1_i = nc.dram_tensor("mx1_i", [1, 128], F32)
    mx1_o = nc.dram_tensor("mx1_o", [1, 128], F32)
    mx2_i = nc.dram_tensor("mx2_i", [1, 128], F32)
    mx2_o = nc.dram_tensor("mx2_o", [1, 128], F32)
    sc_scr = nc.dram_tensor("sc_scr", [1, 8], F32, kind=dk)

    es = contextlib.ExitStack()
    with es:
        def sb(name, shape, dt=F32):
            return es.enter_context(nc.sbuf_tensor(name, shape, dt))

        def ch(name):
            return Ch(es, nc, name)

        biase = sb("biase", [128, 1], F32)
        biasD2e = sb("biasD2e", [128, 1], F32)
        biaslnD = sb("biaslnD", [128, 1], F32)
        biasM = sb("biasM", [128, 1], F32)
        bp_s = sb("bp_s", [128, 12], F32)
        ssq_s = sb("ssq_s", [128, 12], F32)
        mraw_s = sb("mraw_s", [128, 12], F32)
        m_s = sb("m_s", [128, 12], F32)
        m2_s = sb("m2_s", [128, 12], F32)
        var_s = sb("var_s", [128, 12], F32)
        rstd_s = sb("rstd_s", [128, 12], F32)
        lnv_s = sb("lnv_s", [128, 12], F32)
        mx1 = sb("mx1", [128, 1], F32)
        mx2 = sb("mx2", [128, 1], F32)
        mxb4 = sb("mxb4", [128, 4], F32)
        mxbb = sb("mxbb", [128, 1], F32)
        mxb22 = sb("mxb22", [128, NIC], F32)
        mxrow = sb("mxrow", [1, 128], F32)
        scl_x = sb("scl_x", [1, 1], F32)
        scl_d = sb("scl_d", [1, 1], F32)
        e1t = sb("e1t", [1, 1], F32)
        e1t2 = sb("e1t2", [1, 1], F32)
        gxt = sb("gxt", [1, 1], F32)
        gx2t = sb("gx2t", [1, 1], F32)
        t11a = sb("t11a", [1, 1], F32)
        t11b = sb("t11b", [1, 1], F32)
        s1t = sb("s1t", [1, 1], F32)
        is1t = sb("is1t", [1, 1], F32)
        s2t = sb("s2t", [1, 1], F32)
        is2t = sb("is2t", [1, 1], F32)
        bc_s1 = sb("bc_s1", [128, 1], F32)
        bc_is1 = sb("bc_is1", [128, 1], F32)
        bc_s2 = sb("bc_s2", [128, 1], F32)
        ssqF = sb("ssqF", [128, 8], F32)
        mrawF = sb("mrawF", [128, 8], F32)
        m2F = sb("m2F", [128, 8], F32)
        varF = sb("varF", [128, 8], F32)
        lnvF = sb("lnvF", [128, 8], F32)
        mxq4 = sb("mxq4", [128, 4], F32)
        mxb8 = sb("mxb8", [128, 8], F32)
        junk = sb("junk", [128, D], BF16)
        xo_h = [sb(f"xo_h{i}", [128, 512], F32) for i in range(2)]
        rs_h = [sb(f"rs_h{i}", [128, 512], BF16) for i in range(2)]
        yst = sb("yst", [128, D], BF16)
        yld = [sb(f"yld{i}", [128, D], BF16) for i in range(4)]
        xnb1 = [sb(f"xnb1_{i}", [128, D], BF16) for i in range(2)]
        m_arr = sb("m_arr", [128, NOB], F32)
        rstd2_arr = sb("rstd2_arr", [128, NOB], F32)
        rs2_arr = sb("rs2_arr", [128, NOB], F32)
        rmx = sb("rmx", [128, 4], F32)
        rcp = sb("rcp", [128, 4], F32)
        sdd = sb("sdd", [128, 4], F32)
        swg_arr = sb("swg_arr", [128, NIC], F32)
        swu_arr = sb("swu_arr", [128, NIC], F32)
        sgs1 = sb("sgs1", [128, NIC], F32)
        sus1 = sb("sus1", [128, NIC], F32)
        bc_is2 = sb("bc_is2", [128, 1], F32)

        def chp(name, per=16, nring=2):
            return ChP(es, nc, name, per=per, nring=nring)

        s_x = chp("s_x", nring=3); s_xnw = chp("s_xnw"); s_xnr = chp("s_xnr", per=16 * ND)
        s_pw = chp("s_pw")
        s_fw = chp("s_fw", nring=4); s_fww = chp("s_fww", nring=4)
        s_rs = chp("s_rs"); s_xo = chp("s_xo")
        s_xqw = chp("s_xqw"); s_xqr = chp("s_xqr", per=16 * NHC)
        s_hw = chp("s_hw"); s_out = chp("s_out")
        s_hr = chp("s_hr", per=16 * NIC)
        s_clip = ch("s_clip"); s_aw = chp("s_aw", nring=4)
        s_fwr = ch("s_fwr"); s_wdr = ch("s_wdr")
        s_cc = ch("s_cc"); s_mx = ch("s_mx")
        s_gx1 = ch("s_gx1"); s_gx2 = ch("s_gx2")
        s_sc = ch("s_sc")

        c_init = ch("c_init"); c_sq = ch("c_sq"); c_var = ch("c_var")
        c_rstd = ch("c_rstd"); c_xn = ch("c_xn"); c_wcast = ch("c_wcast")
        c_pk = ch("c_pk"); c_pv = ch("c_pv"); c_pq = ch("c_pq")
        c_k = ch("c_k"); c_e = ch("c_e"); c_kvf = ch("c_kvf")
        c_kfree = ch("c_kfree"); c_epk = ch("c_epk"); c_d = ch("c_d")
        c_O = ch("c_O"); c_wc = ch("c_wc"); c_pev = ch("c_pev")
        c_teT = ch("c_teT"); c_wq = ch("c_wq"); c_y = ch("c_y")
        c_f1 = ch("c_f1"); c_e1 = ch("c_e1"); c_s1 = ch("c_s1")
        c_xq = ch("c_xq"); c_pg = ch("c_pg"); c_gu = ch("c_gu")
        c_sg = ch("c_sg"); c_hh = ch("c_hh"); c_mx2 = ch("c_mx2")
        c_s2 = ch("c_s2"); c_h = ch("c_h"); c_dn = ch("c_dn")
        c_oev = ch("c_oev"); c_bp = ch("c_bp"); c_rcp = ch("c_rcp")
        c_wt1 = ch("c_wt1")
        c_sqF = ch("c_sqF"); c_varF = ch("c_varF")
        c_rstdF = ch("c_rstdF"); c_mxF = ch("c_mxF"); c_xb = ch("c_xb")
        s_yw = chp("s_yw"); s_yq = chp("s_yq", nring=3); s_yo = chp("s_yo")
        s_yl = chp("s_yl", nring=4)

        def f1_g(g, t):
            s_yw.wait_item(g, t - 1)
            s_cc.wait(g, CIDX[t] + 1)
            for hf in range(2):
                i2 = 2 * t + hf
                s_xo.inc(g.dma_start(
                    out=xo_h[i2 % 2][:],
                    in_=x_own.ap()[t * 128:(t + 1) * 128,
                                   hf * 512:(hf + 1) * 512]), i2)
                s_rs.inc(g.dma_start(
                    out=rs_h[i2 % 2][:],
                    in_=rs_scr.ap()[t * 128:(t + 1) * 128,
                                    hf * 512:(hf + 1) * 512]), i2)
            for hf in range(2):
                i2 = 2 * t + hf
                s_xo.wait_item(g, i2)
                s_rs.wait_item(g, i2)
                g.tensor_add(yst[:, hf * 512:(hf + 1) * 512],
                             xo_h[i2 % 2][:], rs_h[i2 % 2][:])
            s_yw.inc(g.dma_start(out=y_scr.ap()[t * 128:(t + 1) * 128, :],
                                 in_=yst[:]), t)

        def f1_sa(s, u):
            grF = (u % 2) * 4
            c_varF.wait(s, u - 1)

            for n in range(4):
                t = 4 * u + n
                s_yw.wait_item(s, t)
                s_yl.inc(s.dma_start(out=yld[t % 4][:],
                                     in_=y_scr.ap()[t * 128:(t + 1) * 128, :]),
                         t)
            for n in range(4):
                t = 4 * u + n
                s_yl.wait_item(s, t)
                s.activation(junk[:], yld[t % 4][:], AF.Square, bias=0.0,
                             scale=1.0, accum_out=ssqF[:, grF + n:grF + n + 1])
                c_sqF.inc(s.activation(
                    junk[:], yld[t % 4][:], AF.Identity, bias=0.0, scale=1.0,
                    accum_out=mrawF[:, grF + n:grF + n + 1]), 1)

        def f1_sxb(s, u):
            grF = (u % 2) * 4
            for n in range(4):
                t = 4 * u + n
                c_mxF.wait(s, t - 1)
                c_xb.inc(s.activation(xnb1[t % 2][:], yld[t % 4][:],
                                      AF.Identity, bias=m_arr[:, t:t + 1],
                                      scale=1.0), 1)
            c_varF.wait(s, u + 1)
            s.activation(lnvF[:, grF:grF + 4], varF[:, grF:grF + 4], AF.Ln,
                         bias=biase[:], scale=1.0)
            s.drain()
            c_rstdF.inc(s.activation(
                rstd2_arr[:, 4 * u:4 * u + 4], lnvF[:, grF:grF + 4],
                AF.Exp, bias=0.0, scale=-0.5), 1)

        def f1_da(v, u):
            grF = (u % 2) * 4
            c_sqF.wait(v, 4 * (u + 1))
            v.tensor_scalar(m_arr[:, 4 * u:4 * u + 4], mrawF[:, grF:grF + 4],
                            -1.0 / D, None, ALU.mult)
            c_rstdF.wait(v, u - 1)
            v.drain()
            v.tensor_mul(m2F[:, grF:grF + 4], m_arr[:, 4 * u:4 * u + 4],
                         m_arr[:, 4 * u:4 * u + 4])
            v.drain()
            c_varF.inc(v.scalar_tensor_tensor(
                varF[:, grF:grF + 4], ssqF[:, grF:grF + 4], 1.0 / D,
                m2F[:, grF:grF + 4], ALU.mult, ALU.subtract), 1)

        def f1_dr(v, u):
            grF = (u % 2) * 4
            for n in range(4):
                t = 4 * u + n
                c_xb.wait(v, t + 1)
                c_mxF.inc(v.reduce_max(mxb8[:, grF + n:grF + n + 1],
                                       xnb1[t % 2][:], AX.X,
                                       apply_absolute_value=True), 1)

        def f1_db(v, u):
            grF = (u % 2) * 4
            c_rstdF.wait(v, u + 1)
            v.drain()
            v.tensor_mul(mxq4[:], mxb8[:, grF:grF + 4],
                         rstd2_arr[:, 4 * u:4 * u + 4])
            v.drain()
            v.reduce_max(mxbb[:], mxq4[:], AX.X)
            v.drain()
            ins = v.tensor_max(mx1[:], mx1[:], mxbb[:])
            if u == NQF - 1:
                c_f1.inc(ins, 1)

        if NTT == 16:
            ACT_MAP = {13: [("sa", 0)], 14: [("sx", 0)],
                       15: [("sa", 1)]}
            DVE_MAP = {14: [("da", 0), ("dr", 0)], 15: [("da", 1)]}
            ACT_POST = [("sx", 1), ("sa", 2), ("sx", 2), ("sa", 3),
                        ("sx", 3), ("sa", 4), ("sx", 4), ("sa", 5),
                        ("sx", 5), ("sa", 6), ("sx", 6), ("sa", 7),
                        ("sx", 7)]
            DVE_POST = [("db", 0), ("dr", 1), ("db", 1), ("da", 2), ("dr", 2),
                        ("db", 2), ("da", 3), ("dr", 3), ("db", 3), ("da", 4),
                        ("dr", 4), ("db", 4), ("da", 5), ("dr", 5), ("db", 5),
                        ("da", 6), ("dr", 6), ("db", 6), ("da", 7), ("dr", 7),
                        ("db", 7)]
        else:
            ACT_MAP = {}
            DVE_MAP = {}
            ACT_POST = []
            DVE_POST = []
            for u in range(NQF):
                ACT_POST += [("sa", u), ("sx", u)]
                DVE_POST += [("da", u), ("dr", u), ("db", u)]

        def f1_act(s, items):
            for kind, u in items:
                (f1_sa if kind == "sa" else f1_sxb)(s, u)

        def f1_dve(v, items):
            for kind, u in items:
                {"da": f1_da, "dr": f1_dr, "db": f1_db}[kind](v, u)

        block = es.enter_context(nc.Block())

        sA = contextlib.ExitStack()
        with sA:
            def sba(name, shape, dt=F32):
                return sA.enter_context(nc.sbuf_tensor(name, shape, dt))

            ones_bf = sba("ones_bf", [128, 512], BF16)
            x_t = [sba(f"x_t{i}", [128, 4 * D]) for i in range(3)]
            xnst = [sba(f"xnst{i}", [128, 4 * D], BF16) for i in range(2)]
            xnf = [sba(f"xnf{i}", [128, ND * 512], BF16) for i in range(2)]
            wk_sb = sba("wk_sb", [128, ND * EHW], BF16)
            wv_sb = sba("wv_sb", [128, ND * EHW], BF16)
            wq_sb = sba("wq_sb", [128, ND * EHW], BF16)
            wc_sb = sba("wc_sb", [128, NEC * D], BF16)
            K2 = [sba(f"K2_{i}", [128, 512], BF16) for i in range(2)]
            e2 = [sba(f"e2_{i}", [128, 512], BF16) for i in range(2)]
            KVm = sba("KVm", [128, 512], BF16)
            cumK = sba("cumK", [128, NEC * 512])
            cumKV = sba("cumKV", [128, NEC * 512])
            epk = [sba(f"epk{i}", [128, 512], BF16) for i in range(2)]
            dln = sba("dln", [128, 512])
            dex = [sba(f"dex{i}", [128, 512], BF16) for i in range(2)]
            Obuf = [sba(f"Obuf{i}", [128, NEC * 512], BF16) for i in range(2)]
            pst = [sba(f"pst{i}", [128, D], BF16) for i in range(2)]
            wld = [sba(f"wld{i}", [128, D]) for i in range(4)]
            wqst = [sba(f"wqst{i}", [128, D], BF16) for i in range(4)]

            pk = [sA.enter_context(nc.psum_tensor(f"pk{i}", [128, 512], F32)) for i in range(2)]
            pv = [sA.enter_context(nc.psum_tensor(f"pv{i}", [128, 512], F32)) for i in range(2)]
            pq = [sA.enter_context(nc.psum_tensor(f"pq{i}", [128, 512], F32)) for i in range(2)]
            pwc = [sA.enter_context(nc.psum_tensor(f"pwc{i}", [128, 512], F32)) for i in range(2)]

            AWS = ((wq_d, wq_sb, ND, EHW),
                   (wk_d, wk_sb, ND, EHW),
                   (wv_d, wv_sb, ND, EHW),
                   (wc_d, wc_sb, NEC, D))

            PREP = ([(wg_d, wg_scr, D, "g", ic) for ic in range(NIC)]
                    + [(wu_d, wu_scr, D, "u", ic) for ic in range(NIC)])
            NGU = len(PREP)
            NPQ = (NGU + 3) // 4

            @block.gpsimd
            def _(g):
                s_clip.inc(g.dma_start(out=scl_x[:], in_=clipx_d.ap()), 16)
                s_clip.inc(g.dma_start(out=scl_d[:], in_=clipd_d.ap()), 16)
                for wi, (wdram, _sb, nch, width) in enumerate(AWS):
                    nq_ = nch // 4
                    for qt in range(4):
                        c_wcast.wait(g, 4 * wi + qt - 3)
                        rs_ = wdram.ap()[qt * nq_ * 128:(qt + 1) * nq_ * 128, :]
                        s_aw.inc(g.dma_start(
                            out=sb3(wld[qt][:, 0:nq_ * width], nq_),
                            in_=rows3(rs_)), 4 * wi + qt)

                def rs_chunk(c):
                    sps, spe = SPANS[c]
                    s_pw.wait_upto(g, spe // 128 - 1)
                    s_cc.inc(g.collective_compute(
                        "ReduceScatter", ALU.add, replica_groups=PAIRS,
                        ins=[ps_scr.ap()[sps:spe, :].opt()],
                        outs=[rs_scr.ap()[sps // 2:spe // 2, :].opt()]), 1)

                def prep_load(j):
                    wdram, _scr, cols, kind, row = PREP[j]
                    if j >= 4:
                        c_wq.wait(g, j - 3)
                    else:
                        c_wcast.wait(g, 16)
                    s_fw.inc(g.dma_start(
                        out=wld[j % 4][:, 0:cols],
                        in_=wdram.ap()[row * 128:(row + 1) * 128, :]), j)

                if NTT == 16 and NGU == 44 and NOB == 32:
                    SEQ = ([("j", j) for j in range(16)] + [("c", 0)]
                           + [("j", j) for j in range(16, 44)]
                           + [("f", t) for t in range(0, 8)] + [("c", 1)]
                           + [("f", t) for t in range(8, 12)] + [("c", 2)]
                           + [("f", t) for t in range(12, 20)] + [("c", 3)]
                           + [("f", t) for t in range(20, 24)] + [("c", 4)]
                           + [("f", t) for t in range(24, 32)])
                else:
                    SEQ = ([("j", j) for j in range(NGU)]
                           + [("c", c) for c in range(NCH)]
                           + [("f", t) for t in range(NOB)])
                for kind_, i_ in SEQ:
                    if kind_ == "j":
                        prep_load(i_)
                    elif kind_ == "c":
                        rs_chunk(i_)
                    else:
                        f1_g(g, i_)

            @block.vector
            def _(v):
                v.memset(ones_bf[:], 1.0)
                v.memset(biase[:], EPS_LN)
                v.memset(biasD2e[:], float(D) * float(D) * EPS_LN)
                v.memset(biaslnD[:], float(np.log(D)))
                v.memset(biasM[:], MAGIC)
                v.memset(mx1[:], 0.0)
                v.memset(mx2[:], 0.0)
                v.memset(mxrow[:], 0.0)
                c_init.inc(v.memset(mxbb[:], 0.0), 1)
                for wi, (_w, wsb, nch, width) in enumerate(AWS):
                    qw_ = (nch // 4) * width
                    for qt in range(4):
                        s_aw.wait_item(v, 4 * wi + qt)
                        c_wcast.inc(v.tensor_copy(
                            wsb[:, qt * qw_:(qt + 1) * qw_],
                            wld[qt][:, 0:qw_]), 1)

                def prep_quad(qd):
                    tiles = list(range(4 * qd, min(4 * qd + 4, NGU)))
                    for j in tiles:
                        s_fw.wait_item(v, j)
                        s_fww.wait_item(v, j - 4)
                        v.reduce_max(rmx[:, j % 4:j % 4 + 1],
                                     wld[j % 4][:, 0:PREP[j][2]], AX.X,
                                     apply_absolute_value=True)
                    v.drain()
                    for j in tiles:
                        _w, _scr, cols, kind, row = PREP[j]
                        sd_ = swg_arr[:, row:row + 1] if kind == "g" else swu_arr[:, row:row + 1]
                        v.tensor_scalar(sd_, rmx[:, j % 4:j % 4 + 1],
                                        1.0 / 127.0, None, ALU.mult)
                    v.drain()
                    for j in tiles:
                        _w, _scr, cols, kind, row = PREP[j]
                        sd_ = swg_arr[:, row:row + 1] if kind == "g" else swu_arr[:, row:row + 1]
                        ins_ = v.reciprocal(rcp[:, j % 4:j % 4 + 1], sd_)
                    c_rcp.inc(ins_, 1)
                    for j in tiles:
                        cols_ = PREP[j][2]
                        c_wt1.wait(v, j + 1)
                        c_wq.inc(v.tensor_scalar(
                            wqst[j % 4][:, 0:cols_], wld[j % 4][:, 0:cols_],
                            MAGIC, None, ALU.subtract), 1)

                def stats_xn(tt):
                    gr = (tt % 3) * 4
                    s_x.wait_item(v, tt)
                    s_xnw.wait_item(v, tt - 2)
                    c_sq.wait(v, 4 * (tt + 1))
                    v.tensor_mul(m2_s[:, gr:gr + 4], mraw_s[:, gr:gr + 4],
                                 mraw_s[:, gr:gr + 4])
                    c_rstd.wait(v, tt - 2)
                    v.drain()
                    c_var.inc(v.scalar_tensor_tensor(
                        var_s[:, gr:gr + 4], ssq_s[:, gr:gr + 4], float(D),
                        m2_s[:, gr:gr + 4], ALU.mult, ALU.subtract), 1)
                    c_rstd.wait(v, tt + 1)
                    c_bp.inc(v.scalar_tensor_tensor(
                        bp_s[:, gr:gr + 4], mraw_s[:, gr:gr + 4], -1.0 / D,
                        rstd_s[:, gr:gr + 4], ALU.mult, ALU.mult), 1)
                    v.drain()
                    for n in range(4):
                        c_xn.inc(v.tensor_scalar(
                            xnst[tt % 2][:, n * D:(n + 1) * D],
                            x_t[tt % 3][:, n * D:(n + 1) * D],
                            rstd_s[:, gr + n:gr + n + 1],
                            bp_s[:, gr + n:gr + n + 1],
                            ALU.mult, ALU.add), 1)

                def emit_O(g):
                    tt_, ec_ = g // NEC, g % NEC
                    c_d.wait(v, g + 1)
                    if ec_ == 0 and tt_ >= 2:
                        c_wc.wait(v, 8 * (tt_ - 1))
                    c_O.inc(v.tensor_mul(
                        Obuf[tt_ % 2][:, ec_ * 512:(ec_ + 1) * 512],
                        cumKV[:, ec_ * 512:(ec_ + 1) * 512], dex[g % 2][:]), 1)

                def chain(tt):
                    for ec in range(NEC):
                        g = NEC * tt + ec
                        par = g % 2
                        ck = cumK[:, ec * 512:(ec + 1) * 512]
                        ckv = cumKV[:, ec * 512:(ec + 1) * 512]
                        c_k.wait(v, g + 1)
                        c_pv.wait(v, g + 1)
                        c_kvf.inc(v.tensor_mul(KVm[:], K2[par][:], pv[par][:]), 1)
                        initK = EPS_ATTN if tt == 0 else ck[:, 511:512]
                        c_kfree.inc(v.tensor_tensor_scan(
                            ck, ones_bf[:], K2[par][:], initK, ALU.mult, ALU.add), 1)
                        initV = 0.0 if tt == 0 else ckv[:, 511:512]
                        v.tensor_tensor_scan(ckv, ones_bf[:], KVm[:], initV,
                                             ALU.mult, ALU.add)
                        c_e.wait(v, g + 1)
                        c_epk.inc(v.scalar_tensor_tensor(
                            epk[par][:], e2[par][:], 1.0, ck,
                            ALU.add, ALU.mult), 1)
                        if g >= 1:
                            emit_O(g - 1)

                stats_xn(0)
                if NTT > 1:
                    stats_xn(1)
                for tt in range(NTT):
                    if tt + 2 < NTT:
                        stats_xn(tt + 2)
                    chain(tt)
                    f1_dve(v, DVE_MAP.get(tt, []))
                    if tt < NPQ:
                        prep_quad(tt)
                emit_O(NEC * NTT - 1)
                for qd in range(NTT, NPQ):
                    prep_quad(qd)

            @block.scalar
            def _(s):
                s_clip.wait(s, 32)
                c_e1.inc(s.activation(e1t[:], scl_x[:], AF.Exp, bias=0.0, scale=-1.0), 1)
                c_e1.inc(s.activation(e1t2[:], scl_d[:], AF.Exp, bias=0.0, scale=-1.0), 1)
                c_init.wait(s, 1)

                def stats_act(tt):
                    gr = (tt % 3) * 4
                    s_x.wait_item(s, tt)
                    c_var.wait(s, tt - 2)
                    c_bp.wait(s, tt - 2)
                    for n in range(4):
                        s.activation(junk[:], x_t[tt % 3][:, n * D:(n + 1) * D],
                                     AF.Square, bias=0.0, scale=1.0,
                                     accum_out=ssq_s[:, gr + n:gr + n + 1])
                        c_sq.inc(s.activation(junk[:], x_t[tt % 3][:, n * D:(n + 1) * D],
                                              AF.Identity, bias=0.0, scale=1.0,
                                              accum_out=mraw_s[:, gr + n:gr + n + 1]), 1)
                    c_var.wait(s, tt + 1)
                    s.activation(lnv_s[:, gr:gr + 4], var_s[:, gr:gr + 4],
                                 AF.Ln, bias=biasD2e[:], scale=1.0)
                    s.drain()
                    c_rstd.inc(s.activation(rstd_s[:, gr:gr + 4], lnv_s[:, gr:gr + 4],
                                            AF.Exp, bias=biaslnD[:], scale=-0.5), 1)


                def emit_lnd(g):
                    c_epk.wait(s, g + 1)
                    s.activation(dln[:], epk[g % 2][:], AF.Ln, bias=0.0, scale=1.0)
                    if g >= 2:
                        c_O.wait(s, g - 1)
                    c_d.inc(s.activation(dex[g % 2][:], dln[:], AF.Exp,
                                         bias=0.0, scale=-1.0), 1)

                def chain_act(tt):
                    for ec in range(NEC):
                        g = NEC * tt + ec
                        par = g % 2
                        c_pk.wait(s, g + 1)
                        if g >= 2:
                            c_kfree.wait(s, g - 1)
                        c_k.inc(s.activation(K2[par][:], pk[par][:], AF.Exp,
                                             bias=0.0, scale=1.0), 1)
                        c_pq.wait(s, g + 1)
                        if g >= 2:
                            c_epk.wait(s, g - 1)
                        c_e.inc(s.activation(e2[par][:], pq[par][:], AF.Exp,
                                             bias=0.0, scale=-1.0), 1)
                        if g >= 1:
                            emit_lnd(g - 1)

                def evac_pdma(tt):
                    def dma_slice(slc):
                        q = 4 * tt + slc
                        t0 = q * 128
                        row = ps2_row(t0)
                        s_pw.inc(s.dma_start(
                            out=ps_scr.ap()[row:row + 128, :],
                            in_=pst[q % 2][:]), q)

                    for slc in range(4):
                        q = 4 * tt + slc
                        s_pw.wait_item(s, q - 2)
                        for hf in range(2):
                            idx = 8 * tt + 2 * slc + hf
                            c_wc.wait(s, idx + 1)
                            c_pev.inc(s.copy(pst[q % 2][:, hf * 512:(hf + 1) * 512],
                                             pwc[idx % 2][:]), 1)
                        if slc >= 1:
                            dma_slice(slc - 1)
                    dma_slice(3)

                def prep_act(qd):
                    c_rcp.wait(s, qd + 1)
                    for j in range(4 * qd, min(4 * qd + 4, NGU)):
                        _w, _scr, cols, kind, row = PREP[j]
                        c_wt1.inc(s.activation(
                            wld[j % 4][:, 0:cols], wld[j % 4][:, 0:cols],
                            AF.Identity, bias=biasM[:],
                            scale=rcp[:, j % 4:j % 4 + 1]), 1)

                stats_act(0)
                if NTT > 1:
                    stats_act(1)
                for tt in range(NTT):
                    if tt + 2 < NTT:
                        stats_act(tt + 2)
                    chain_act(tt)
                    if tt >= 1:
                        evac_pdma(tt - 1)
                    f1_act(s, ACT_MAP.get(tt, []))
                    if tt < NPQ:
                        prep_act(tt)
                emit_lnd(NEC * NTT - 1)
                evac_pdma(NTT - 1)
                for qd in range(NTT, NPQ):
                    prep_act(qd)

            @block.sync
            def _(sy):
                for tt0 in range(min(3, NTT)):
                    src = rows3(xb.ap()[tt0 * 512:(tt0 + 1) * 512, :])
                    s_x.inc(sy.dma_start(out=sb3(x_t[tt0 % 3][:], 4), in_=src), tt0)
                for tt in range(NTT):
                    c_xn.wait(sy, 4 * (tt + 1))
                    s_xnw.inc(sy.dma_start(
                        out=rows3(xn_scr.ap()[tt * 512:(tt + 1) * 512, :]),
                        in_=sb3(xnst[tt % 2][:], 4)), tt)
                    s_xnw.wait_item(sy, tt)
                    c_teT.wait(sy, tt - 2)
                    for dc in range(ND):
                        s_xnr.inc(sy.dma_start(
                            out=xnf[tt % 2][:, dc * 512:(dc + 1) * 512],
                            in_=xn_scr.ap()[tt * 512:(tt + 1) * 512,
                                            dc * 128:(dc + 1) * 128],
                            transpose=True), tt)
                    if tt + 3 < NTT:
                        c_xn.wait(sy, 4 * (tt + 1))
                        c_sq.wait(sy, 4 * (tt + 1))
                        src = rows3(xb.ap()[(tt + 3) * 512:(tt + 4) * 512, :])
                        s_x.inc(sy.dma_start(out=sb3(x_t[tt % 3][:], 4), in_=src),
                                tt + 3)
                    if 1 <= tt and tt - 1 < NPQ:
                        for j in range(4 * (tt - 1), min(4 * tt, NGU)):
                            _w, scr, cols, kind, row = PREP[j]
                            c_wq.wait(sy, j + 1)
                            s_fww.inc(sy.dma_start(
                                out=scr.ap()[row * 128:(row + 1) * 128, :],
                                in_=wqst[j % 4][:, 0:cols]), j)
                for qd in range(min(NTT - 1, NPQ), NPQ):
                    for j in range(4 * qd, min(4 * qd + 4, NGU)):
                        _w, scr, cols, kind, row = PREP[j]
                        c_wq.wait(sy, j + 1)
                        s_fww.inc(sy.dma_start(
                            out=scr.ap()[row * 128:(row + 1) * 128, :],
                            in_=wqst[j % 4][:, 0:cols]), j)

            @block.tensor
            def _(te):
                c_wcast.wait(te, 4)

                def proj(tt):
                    s_xnr.wait_item(te, tt)
                    for ec in range(NEC):
                        g = NEC * tt + ec
                        par = g % 2
                        if g >= 2:
                            c_k.wait(te, g - 1)
                            c_kvf.wait(te, g - 1)
                            c_e.wait(te, g - 1)
                        for wsb, psm, chn in ((wk_sb, pk, c_pk), (wv_sb, pv, c_pv),
                                              (wq_sb, pq, c_pq)):
                            for dc in range(ND):
                                last = te.matmul(
                                    psm[par][:],
                                    wsb[:, dc * EHW + ec * 128: dc * EHW + (ec + 1) * 128],
                                    xnf[tt % 2][:, dc * 512:(dc + 1) * 512],
                                    start=(dc == 0), stop=(dc == ND - 1))
                            chn.inc(last, 1)

                def wc(tt):
                    c_O.wait(te, NEC * (tt + 1))
                    first_wc = True
                    for slc in range(4):
                        for hf in range(2):
                            idx = 8 * tt + 2 * slc + hf
                            if idx >= 2:
                                c_pev.wait(te, idx - 1)
                            for ec in range(NEC):
                                last = te.matmul(
                                    pwc[idx % 2][:],
                                    Obuf[tt % 2][:, ec * 512 + slc * 128:
                                                  ec * 512 + (slc + 1) * 128],
                                    wc_sb[:, ec * D + hf * 512: ec * D + (hf + 1) * 512],
                                    start=(ec == 0), stop=(ec == NEC - 1))
                                if first_wc and ec == 0:
                                    c_teT.inc(last, 1)
                                    first_wc = False
                            c_wc.inc(last, 1)

                for tt in range(NTT):
                    proj(tt)
                    if tt >= 1:
                        wc(tt - 1)
                wc(NTT - 1)

        sFF = contextlib.ExitStack()
        with sFF:
            def sbf(name, shape, dt=F32):
                return sFF.enter_context(nc.sbuf_tensor(name, shape, dt))


            if ACT_POST:
                @block.scalar
                def _(s):
                    f1_act(s, ACT_POST)
            if DVE_POST:
                @block.vector
                def _(v):
                    f1_dve(v, DVE_POST)

            sWG = contextlib.ExitStack()
            with sWG:
                wgT = sWG.enter_context(nc.sbuf_tensor("wgT", [128, NHC * DI], BF16))
                wuT = sWG.enter_context(nc.sbuf_tensor("wuT", [128, NHC * DI], BF16))

                @block.sync
                def _(sy):
                    s_pw.wait_upto(sy, 4 * NTT - 1)
                    c_pev.wait(sy, 8 * NTT)
                    c_wc.wait(sy, 8 * NTT)
                    s_xnw.wait_upto(sy, NTT - 1)
                    s_fww.wait_upto(sy, NGU - 1)
                    for wi, (scr, dst) in enumerate(((wg_scr, wgT), (wu_scr, wuT))):
                        for hc in range(NHC):
                            s_fwr.inc(sy.dma_start(
                                out=dst[:, hc * DI:(hc + 1) * DI],
                                in_=scr.ap()[0:DI, hc * 128:(hc + 1) * 128],
                                transpose=True), 16)

                sPD = contextlib.ExitStack()
                with sPD:
                    wldd = [sPD.enter_context(nc.sbuf_tensor(f"wldd{i}", [128, DI], F32))
                            for i in range(2)]
                    wqsd = [sPD.enter_context(nc.sbuf_tensor(f"wqsd{i}", [128, DI], BF16))
                            for i in range(2)]

                    @block.gpsimd
                    def _(g):
                        for hc in range(NHC):
                            if hc >= 2:
                                c_wq.wait(g, NGU + hc - 1)
                            else:
                                s_pw.wait_upto(g, 4 * NTT - 1)
                                c_pev.wait(g, 8 * NTT)
                                c_wc.wait(g, 8 * NTT)
                            s_fw.inc(g.dma_start(
                                out=wldd[hc % 2][:],
                                in_=wd_d.ap()[hc * 128:(hc + 1) * 128, :]), NGU + hc)

                    @block.vector
                    def _(v):
                        for p2 in range(NHC // 2):
                            hcs = [2 * p2, 2 * p2 + 1]
                            for hc in hcs:
                                s_fw.wait_item(v, NGU + hc)
                                s_fww.wait_item(v, NGU + hc - 2)
                                v.reduce_max(rmx[:, hc % 2:hc % 2 + 1] if False else sdd[:, hc % 2:hc % 2 + 1],
                                             wldd[hc % 2][:], AX.X,
                                             apply_absolute_value=True)
                            v.drain()
                            for hc in hcs:
                                v.tensor_scalar(sdd[:, 2 + hc % 2:3 + hc % 2],
                                                sdd[:, hc % 2:hc % 2 + 1],
                                                1.0 / 127.0, None, ALU.mult)
                            v.drain()
                            for hc in hcs:
                                v.reciprocal(rcp[:, hc % 2:hc % 2 + 1],
                                             sdd[:, 2 + hc % 2:3 + hc % 2])
                            v.drain()
                            for hc in hcs:
                                v.tensor_scalar(wldd[hc % 2][:], wldd[hc % 2][:],
                                                rcp[:, hc % 2:hc % 2 + 1], MAGIC,
                                                ALU.mult, ALU.add)
                                v.tensor_scalar(wldd[hc % 2][:], wldd[hc % 2][:],
                                                MAGIC, None, ALU.subtract)
                                c_wq.inc(v.tensor_scalar(
                                    wqsd[hc % 2][:], wldd[hc % 2][:],
                                    sdd[:, 2 + hc % 2:3 + hc % 2], None, ALU.mult), 1)

                    @block.scalar
                    def _(s):
                        for hc in range(NHC):
                            c_wq.wait(s, NGU + hc + 1)
                            s_fww.inc(s.dma_start(
                                out=wd_scr.ap()[hc * 128:(hc + 1) * 128, :],
                                in_=wqsd[hc % 2][:]), NGU + hc)

                sF1 = contextlib.ExitStack()
                with sF1:
                    @block.gpsimd
                    def _(g):
                        c_f1.wait(g, 1)
                        s_mx.inc(g.dma_start(out=mx1_i.ap(), in_=mx1[:]), 16)
                        g.wait_ge(s_mx.sem, 16)
                        s_cc.inc(g.collective_compute(
                            "AllReduce", ALU.max, replica_groups=ALL8,
                            ins=[mx1_i.ap().opt()], outs=[mx1_o.ap().opt()]), 1)

                    @block.sync
                    def _(sy):
                        s_cc.wait(sy, NCH + 1)
                        s_gx1.inc(sy.dma_start(out=mxrow[:], in_=mx1_o.ap()), 16)
                        s_sc.wait(sy, 32)
                        s_gx1.inc(sy.dma_start(out=bc_s1[:], in_=bcast_ap(sc_scr, 0)), 16)
                        s_gx1.inc(sy.dma_start(out=bc_is1[:], in_=bcast_ap(sc_scr, 1)), 16)

                    @block.vector
                    def _(v):
                        c_e1.wait(v, 1)
                        s_gx1.wait(v, 16)
                        v.reduce_max(gxt[:], mxrow[0:1, :], AX.X)
                        v.drain()
                        v.tensor_scalar(t11a[:], e1t[:], 1.0, None, ALU.add)
                        v.drain()
                        v.reciprocal(t11b[:], t11a[:])
                        v.drain()
                        v.tensor_scalar(s1t[:], t11b[:], gxt[0:1, 0:1], 1.0 / 127.0,
                                        ALU.mult, ALU.mult)
                        v.drain()
                        c_s1.inc(v.reciprocal(is1t[:], s1t[:]), 1)
                        s_gx1.wait(v, 48)
                        v.tensor_scalar(rs2_arr[:], rstd2_arr[:], bc_is1[:, 0:1],
                                        None, ALU.mult)
                        v.tensor_scalar(sgs1[:], swg_arr[:], bc_s1[:, 0:1],
                                        None, ALU.mult)
                        c_s1.inc(v.tensor_scalar(sus1[:], swu_arr[:], bc_s1[:, 0:1],
                                                 None, ALU.mult), 1)
                        v.drain()

                    @block.scalar
                    def _(s):
                        c_s1.wait(s, 1)
                        s_sc.inc(s.dma_start(out=sc_scr.ap()[0:1, 0:1], in_=s1t[:]), 16)
                        s_sc.inc(s.dma_start(out=sc_scr.ap()[0:1, 1:2], in_=is1t[:]), 16)

                sF2 = contextlib.ExitStack()
                with sF2:
                    xqst = [sF2.enter_context(nc.sbuf_tensor(f"xqst{i}", [128, 4 * D], BF16))
                            for i in range(2)]
                    yq = [sF2.enter_context(nc.sbuf_tensor(f"yq{i}", [128, D], BF16))
                          for i in range(3)]
                    xqf = [sF2.enter_context(nc.sbuf_tensor(f"xqf{i}", [128, NHC * 512], BF16))
                           for i in range(2)]
                    sgb = [sF2.enter_context(nc.sbuf_tensor(f"sgb{i}", [128, 512], BF16))
                           for i in range(2)]
                    hhst = [sF2.enter_context(nc.sbuf_tensor(f"hhst{i}", [128, 512], BF16))
                            for i in range(2)]
                    xnb2 = sF2.enter_context(nc.sbuf_tensor("xnb2", [128, D], F32))
                    pg = [sF2.enter_context(nc.psum_tensor(f"pg{i}", [128, 512], F32))
                          for i in range(2)]
                    pu = [sF2.enter_context(nc.psum_tensor(f"pu{i}", [128, 512], F32))
                          for i in range(2)]

                    @block.vector
                    def _(v):
                        def quant(gq):
                            for n in range(4):
                                t = 4 * gq + n
                                if n == 0:
                                    s_xqw.wait_item(v, gq - 2)
                                s_yq.wait_item(v, t)
                                v.tensor_scalar(xnb2[:], yq[t % 3][:],
                                                m_arr[:, t:t + 1],
                                                rs2_arr[:, t:t + 1],
                                                ALU.add, ALU.mult)
                                v.tensor_scalar(xnb2[:], xnb2[:], MAGIC, MAGIC,
                                                ALU.add, ALU.subtract)
                                c_xq.inc(v.tensor_scalar(
                                    xqst[gq % 2][:, n * D:(n + 1) * D], xnb2[:],
                                    127.0, -128.0, ALU.min, ALU.max), 1)

                        def hhwork(gq):
                            for ic in range(NIC):
                                gi = NIC * gq + ic
                                c_sg.wait(v, gi + 1)
                                c_gu.wait(v, gi + 1)
                                s_hw.wait_item(v, gi - 2)
                                c_hh.inc(v.scalar_tensor_tensor(
                                    hhst[gi % 2][:], pu[gi % 2][:],
                                    sus1[:, ic:ic + 1], sgb[gi % 2][:],
                                    ALU.mult, ALU.mult), 1)
                                v.reduce_max(mxb22[:, ic:ic + 1], hhst[gi % 2][:],
                                             AX.X, apply_absolute_value=True)
                            v.drain()
                            v.reduce_max(mxbb[:], mxb22[:], AX.X)
                            v.drain()
                            ins = v.tensor_max(mx2[:], mx2[:], mxbb[:])
                            if gq == NG - 1:
                                c_mx2.inc(ins, 1)

                        quant(0)
                        for gq in range(NG):
                            if gq + 1 < NG:
                                quant(gq + 1)
                            hhwork(gq)

                    @block.sync
                    def _(sy):
                        def yql(t):
                            s_yw.wait_upto(sy, t)
                            if t >= 3:
                                c_xq.wait(sy, t - 2)
                            s_yq.inc(sy.dma_start(
                                out=yq[t % 3][:],
                                in_=y_scr.ap()[t * 128:(t + 1) * 128, :]), t)

                        for t0 in range(min(3, NOB)):
                            yql(t0)
                        for gq in range(NG):
                            for n in range(4):
                                if 4 * gq + n + 3 < NOB:
                                    yql(4 * gq + n + 3)
                            c_xq.wait(sy, 4 * (gq + 1))
                            s_xqw.inc(sy.dma_start(
                                out=rows3(xq_scr.ap()[gq * 512:(gq + 1) * 512, :]),
                                in_=sb3(xqst[gq % 2][:], 4)), gq)
                            s_xqw.wait_item(sy, gq)
                            if gq >= 2:
                                c_gu.wait(sy, NIC * (gq - 1))
                            for hc in range(NHC):
                                s_xqr.inc(sy.dma_start(
                                    out=xqf[gq % 2][:, hc * 512:(hc + 1) * 512],
                                    in_=xq_scr.ap()[gq * 512:(gq + 1) * 512,
                                                    hc * 128:(hc + 1) * 128],
                                    transpose=True), gq)

                    @block.tensor
                    def _(te):
                        s_fwr.wait(te, 16 * 2 * NHC)
                        c_pev.wait(te, 8 * NTT)
                        c_k.wait(te, NEC * NTT)
                        c_e.wait(te, NEC * NTT)
                        c_kvf.wait(te, NEC * NTT)
                        for gq in range(NG):
                            s_xqr.wait_item(te, gq)
                            for ic in range(NIC):
                                gi = NIC * gq + ic
                                par = gi % 2
                                if gi >= 2:
                                    c_sg.wait(te, gi - 1)
                                    c_hh.wait(te, gi - 1)
                                for wT, psm, chn in ((wgT, pg, c_pg), (wuT, pu, c_gu)):
                                    for hc in range(NHC):
                                        last = te.matmul(
                                            psm[par][:],
                                            wT[:, hc * DI + ic * 128: hc * DI + (ic + 1) * 128],
                                            xqf[gq % 2][:, hc * 512:(hc + 1) * 512],
                                            start=(hc == 0), stop=(hc == NHC - 1))
                                    chn.inc(last, 1)

                    @block.scalar
                    def _(s):
                        c_s1.wait(s, 2)
                        for gq in range(NG):
                            for ic in range(NIC):
                                gi = NIC * gq + ic
                                c_pg.wait(s, gi + 1)
                                if gi >= 2:
                                    c_hh.wait(s, gi - 1)
                                c_sg.inc(s.activation(sgb[gi % 2][:], pg[gi % 2][:],
                                                      AF.Silu, bias=0.0,
                                                      scale=sgs1[:, ic:ic + 1]), 1)
                                if gi >= 1:
                                    pj = gi - 1
                                    c_hh.wait(s, pj + 1)
                                    s_hw.inc(s.dma_start(
                                        out=hh_scr.ap()[(pj % NIC) * 128:(pj % NIC + 1) * 128,
                                                        (pj // NIC) * 512:(pj // NIC + 1) * 512],
                                        in_=hhst[pj % 2][:]), pj)
                        pj = NIC * NG - 1
                        c_hh.wait(s, pj + 1)
                        s_hw.inc(s.dma_start(
                            out=hh_scr.ap()[(pj % NIC) * 128:(pj % NIC + 1) * 128,
                                            (pj // NIC) * 512:(pj // NIC + 1) * 512],
                            in_=hhst[pj % 2][:]), pj)

                    @block.gpsimd
                    def _(g):
                        c_mx2.wait(g, 1)
                        s_mx.inc(g.dma_start(out=mx2_i.ap(), in_=mx2[:]), 16)
                        g.wait_ge(s_mx.sem, 32)
                        s_cc.inc(g.collective_compute(
                            "AllReduce", ALU.max, replica_groups=ALL8,
                            ins=[mx2_i.ap().opt()], outs=[mx2_o.ap().opt()]), 1)

                    @block.vector
                    def _(v):
                        c_e1.wait(v, 2)
                        s_gx2.wait(v, 16)
                        v.reduce_max(gx2t[:], mxrow[0:1, :], AX.X)
                        v.drain()
                        v.tensor_scalar(t11a[:], e1t2[:], 1.0, None, ALU.add)
                        v.drain()
                        v.reciprocal(t11b[:], t11a[:])
                        v.drain()
                        v.tensor_scalar(s2t[:], t11b[:], gx2t[0:1, 0:1], 1.0 / 127.0,
                                        ALU.mult, ALU.mult)
                        v.drain()
                        c_s2.inc(v.reciprocal(is2t[:], s2t[:]), 1)

                    @block.sync
                    def _(sy):
                        s_cc.wait(sy, NCH + 2)
                        s_gx2.inc(sy.dma_start(out=mxrow[:], in_=mx2_o.ap()), 16)

                    @block.scalar
                    def _(s):
                        c_s2.wait(s, 1)
                        s_sc.inc(s.dma_start(out=sc_scr.ap()[0:1, 2:3], in_=s2t[:]), 16)
                        s_sc.inc(s.dma_start(out=sc_scr.ap()[0:1, 3:4], in_=is2t[:]), 16)

            sF3 = contextlib.ExitStack()
            with sF3:
                wdT = sF3.enter_context(nc.sbuf_tensor("wdT", [128, NIC * D], BF16))
                hhld = [sF3.enter_context(nc.sbuf_tensor(f"hhld{i}", [128, NIC * 512], BF16))
                        for i in range(2)]
                hbuf = [sF3.enter_context(nc.sbuf_tensor(f"hbuf{i}", [128, NIC * 512], BF16))
                        for i in range(2)]
                tqa = sF3.enter_context(nc.sbuf_tensor("tqa", [128, 512], F32))
                outst = [sF3.enter_context(nc.sbuf_tensor(f"outst{i}", [128, 512], F32))
                         for i in range(2)]
                yo = [sF3.enter_context(nc.sbuf_tensor(f"yo{i}", [128, 4 * D], BF16))
                      for i in range(2)]
                pd = [sF3.enter_context(nc.psum_tensor(f"pd{i}", [128, 512], F32))
                      for i in range(8)]

                @block.scalar
                def _(s):
                    c_gu.wait(s, NIC * NG)
                    c_hh.wait(s, NIC * NG)
                    s_hw.wait_upto(s, NIC * NG - 1)
                    s_xqw.wait_upto(s, NG - 1)
                    s_fww.wait_upto(s, NW - 1)
                    for ic in range(NIC):
                        s_wdr.inc(s.dma_start(
                            out=wdT[:, ic * D:(ic + 1) * D],
                            in_=wd_scr.ap()[0:D, ic * 128:(ic + 1) * 128],
                            transpose=True), 16)

                @block.sync
                def _(sy):
                    c_gu.wait(sy, NIC * NG)
                    c_hh.wait(sy, NIC * NG)

                    def hld(gq):
                        s_hw.wait_upto(sy, NIC * (gq + 1) - 1)
                        if gq >= 2:
                            c_h.wait(sy, NIC * (gq - 1))
                            c_oev.wait(sy, 8 * (gq - 1))
                        for ic in range(NIC):
                            s_hr.inc(sy.dma_start(
                                out=hhld[gq % 2][:, ic * 512:(ic + 1) * 512],
                                in_=hh_scr.ap()[ic * 128:(ic + 1) * 128,
                                                gq * 512:(gq + 1) * 512]), gq)
                        s_yo.inc(sy.dma_start(
                            out=sb3(yo[gq % 2][:], 4),
                            in_=rows3(y_scr.ap()[gq * 512:(gq + 1) * 512, :])), gq)

                    for gq in range(min(2, NG)):
                        hld(gq)
                    s_sc.wait(sy, 64)
                    s_gx2.inc(sy.dma_start(out=bc_s2[:], in_=bcast_ap(sc_scr, 2)), 16)
                    s_gx2.inc(sy.dma_start(out=bc_is2[:], in_=bcast_ap(sc_scr, 3)), 16)
                    for gq in range(2, NG):
                        hld(gq)
                    s_out.wait_upto(sy, 2 * NOB - 1)

                @block.vector
                def _(v):
                    s_gx2.wait(v, 48)

                    def quant(gq):
                        for ic in range(NIC):
                            gi = NIC * gq + ic
                            if ic == 0:
                                s_hr.wait_item(v, gq)
                                if gq >= 2:
                                    c_dn.wait(v, 8 * (gq - 1))
                            v.tensor_scalar(tqa[:], hhld[gq % 2][:, ic * 512:(ic + 1) * 512],
                                            bc_is2[:, 0:1], MAGIC, ALU.mult, ALU.add)
                            v.tensor_scalar(tqa[:], tqa[:], MAGIC, 127.0,
                                            ALU.subtract, ALU.min)
                            c_h.inc(v.tensor_scalar(
                                hbuf[gq % 2][:, ic * 512:(ic + 1) * 512], tqa[:],
                                -128.0, None, ALU.max), 1)

                    def outevac(gq, h):
                        if h == 0:
                            s_yo.wait_item(v, gq)
                        for n in (2 * h, 2 * h + 1):
                            t = 4 * gq + n
                            for hf in range(2):
                                idx = 2 * t + hf
                                s_out.wait_item(v, idx - 2)
                                c_dn.wait(v, idx + 1)
                                c_oev.inc(v.scalar_tensor_tensor(
                                    outst[idx % 2][:],
                                    pd[idx % 8][:], bc_s2[:, 0:1],
                                    yo[gq % 2][:, n * D + hf * 512:
                                               n * D + (hf + 1) * 512],
                                    ALU.mult, ALU.add), 1)

                    quant(0)
                    for gq in range(NG):
                        if gq + 1 < NG:
                            outevac(gq, 0)
                            quant(gq + 1)
                            outevac(gq, 1)
                        else:
                            outevac(gq, 0)
                            outevac(gq, 1)

                @block.tensor
                def _(te):
                    s_wdr.wait(te, 16 * NIC)
                    c_sg.wait(te, NIC * NG)
                    c_hh.wait(te, NIC * NG)
                    for gq in range(NG):
                        for h in range(2):
                            if gq >= 1:
                                c_oev.wait(te, 8 * (gq - 1) + 4 * (h + 1))
                            for ic in range(NIC):
                                if h == 0:
                                    c_h.wait(te, NIC * gq + ic + 1)
                                for n in (2 * h, 2 * h + 1):
                                    t = 4 * gq + n
                                    for hf in range(2):
                                        idx = 2 * t + hf
                                        last = te.matmul(
                                            pd[idx % 8][:],
                                            hbuf[gq % 2][:, ic * 512 + n * 128:
                                                          ic * 512 + (n + 1) * 128],
                                            wdT[:, ic * D + hf * 512: ic * D + (hf + 1) * 512],
                                            start=(ic == 0), stop=(ic == NIC - 1))
                                        if ic == NIC - 1:
                                            c_dn.inc(last, 1)

                @block.scalar
                def _(s):
                    for t in range(NOB):
                        for hf in range(2):
                            idx = 2 * t + hf
                            c_oev.wait(s, idx + 1)
                            s_out.inc(s.dma_start(
                                out=out_d.ap()[t * 128:(t + 1) * 128,
                                               hf * 512:(hf + 1) * 512],
                                in_=outst[idx % 2][:]), idx)

    return nc


_NC_CACHE = {}


def _get_nc(LQ, D, DI):
    key = (LQ, D, DI)
    if key not in _NC_CACHE:
        _NC_CACHE[key] = build_nc(LQ, D, DI)
    return _NC_CACHE[key]


def kernel(L0_attn_X, L0_attn_Wq, L0_attn_Wk, L0_attn_Wv, L0_attn_Wc,
           L0_ffn_clip_x, L0_ffn_clip_down, L0_ffn_w_gate, L0_ffn_w_up,
           L0_ffn_w_down):
    X = np.ascontiguousarray(np.asarray(L0_attn_X, dtype=np.float32))
    Wq = np.ascontiguousarray(np.asarray(L0_attn_Wq, dtype=np.float32))
    Wk = np.ascontiguousarray(np.asarray(L0_attn_Wk, dtype=np.float32))
    Wv = np.ascontiguousarray(np.asarray(L0_attn_Wv, dtype=np.float32))
    Wc = np.ascontiguousarray(np.asarray(L0_attn_Wc, dtype=np.float32))
    Wg = np.ascontiguousarray(np.asarray(L0_ffn_w_gate, dtype=np.float32))
    Wu = np.ascontiguousarray(np.asarray(L0_ffn_w_up, dtype=np.float32))
    Wd = np.ascontiguousarray(np.asarray(L0_ffn_w_down, dtype=np.float32))
    cx = np.asarray(L0_ffn_clip_x, dtype=np.float32).reshape(1, 1)
    cd = np.asarray(L0_ffn_clip_down, dtype=np.float32).reshape(1, 1)

    B, LQ, D = X.shape
    DI = Wg.shape[0]
    EHW = D // 2
    q4 = LQ // 4
    SPANS = [(0, q4), (q4, 2 * q4), (2 * q4, 3 * q4),
             (3 * q4, 3 * q4 + q4 // 2), (3 * q4 + q4 // 2, 4 * q4)]

    def own_rows(eh):
        return np.concatenate(
            [np.arange(s + eh * (e - s) // 2, s + (eh + 1) * (e - s) // 2)
             for s, e in SPANS])

    nc = _get_nc(LQ, D, DI)
    in_maps = []
    for c in range(8):
        b, eh = c >> 1, c & 1
        in_maps.append(dict(
            xb=X[b],
            x_own=np.ascontiguousarray(X[b][own_rows(eh)]),
            wq=np.ascontiguousarray(Wq[:, eh * EHW:(eh + 1) * EHW]),
            wk=np.ascontiguousarray(Wk[:, eh * EHW:(eh + 1) * EHW]),
            wv=np.ascontiguousarray(Wv[:, eh * EHW:(eh + 1) * EHW]),
            wc=np.ascontiguousarray(Wc[eh * EHW:(eh + 1) * EHW, :]),
            wg=Wg, wu=Wu, wd=Wd, clipx=cx, clipd=cd,
        ))
    res = run_bass_kernel_spmd(nc, in_maps, core_ids=list(range(8)))
    out = np.empty((B, LQ, D), dtype=np.float32)
    for c in range(8):
        b, eh = c >> 1, c & 1
        out[b, own_rows(eh)] = np.asarray(res.results[c]["out"])
    return out


# version 21 (final)
# speedup vs baseline: 1.0180x; 1.0180x over previous
import contextlib
import numpy as np

import concourse.bass as bass
import concourse.mybir as mybir
from concourse.bass_utils import run_bass_kernel_spmd

F32 = mybir.dt.float32
BF16 = mybir.dt.bfloat16
AF = mybir.ActivationFunctionType
ALU = mybir.AluOpType
AX = mybir.AxisListType

MAGIC = float(np.float32(3 * 2**22))
EPS_LN = 1e-5
EPS_ATTN = 1e-6
PAIRS = [[0, 1], [2, 3], [4, 5], [6, 7]]
ALL8 = [list(range(8))]


class Ch:
    def __init__(self, es, nc, name):
        self.sem = es.enter_context(nc.semaphore(name))
        self.n = 0

    def inc(self, instr, k=1):
        instr.then_inc(self.sem, k)
        self.n += k
        return instr

    def wait(self, eng, v):
        if v > 0:
            eng.wait_ge(self.sem, v)


class ChP:

    def __init__(self, es, nc, name, per=16, nring=2):
        self.s = [es.enter_context(nc.semaphore(f"{name}{i}"))
                  for i in range(nring)]
        self.per = per
        self.nr = nring

    def inc(self, instr, item, k=16):
        instr.then_inc(self.s[item % self.nr], k)
        return instr

    def wait_item(self, eng, item):
        if item < 0:
            return
        eng.wait_ge(self.s[item % self.nr], self.per * (item // self.nr + 1))

    def wait_upto(self, eng, item):
        for i in range(self.nr):
            self.wait_item(eng, item - i)


def bcast_ap(handle, off):
    return bass.AP(handle, off, [[0, 128], [1, 1]])


def rows3(ap2d):
    return ap2d.rearrange("(n p) d -> p n d", p=128)


def sb3(ap2d, n):
    return ap2d.rearrange("p (n d) -> p n d", n=n)


def build_nc(LQ=8192, D=1024, DI=2816, debug=False):
    TOK = LQ // 2
    NTT = LQ // 512
    NA = 4 * NTT
    ND = D // 128
    EHW = D // 2
    NEC = EHW // 128
    NIC = DI // 128
    NHC = D // 128
    NG = TOK // 512
    NOB = TOK // 128
    NW = 2 * NIC + NHC
    NQF = NOB // 4
    q4 = LQ // 4
    SPANS = [(0, q4), (q4, 2 * q4), (2 * q4, 3 * q4),
             (3 * q4, 3 * q4 + q4 // 2), (3 * q4 + q4 // 2, 4 * q4)]
    NCH = len(SPANS)
    CIDX = []
    for _t in range(NOB):
        for _ci, (_s, _e) in enumerate(SPANS):
            if _t * 128 < _e // 2:
                CIDX.append(_ci)
                break

    def ps2_row(t0):
        return t0

    nc = bass.Bass("TRN2", target_bir_lowering=False,
                   detect_race_conditions=False)

    xb = nc.dram_tensor("xb", [LQ, D], F32, kind="ExternalInput")
    x_own = nc.dram_tensor("x_own", [TOK, D], F32, kind="ExternalInput")
    wq_d = nc.dram_tensor("wq", [D, EHW], F32, kind="ExternalInput")
    wk_d = nc.dram_tensor("wk", [D, EHW], F32, kind="ExternalInput")
    wv_d = nc.dram_tensor("wv", [D, EHW], F32, kind="ExternalInput")
    wc_d = nc.dram_tensor("wc", [EHW, D], F32, kind="ExternalInput")
    wg_d = nc.dram_tensor("wg", [DI, D], F32, kind="ExternalInput")
    wu_d = nc.dram_tensor("wu", [DI, D], F32, kind="ExternalInput")
    wd_d = nc.dram_tensor("wd", [D, DI], F32, kind="ExternalInput")
    clipx_d = nc.dram_tensor("clipx", [1, 1], F32, kind="ExternalInput")
    clipd_d = nc.dram_tensor("clipd", [1, 1], F32, kind="ExternalInput")
    out_d = nc.dram_tensor("out", [TOK, D], F32, kind="ExternalOutput")

    dk = "ExternalOutput" if debug else "Internal"
    xn_scr = nc.dram_tensor("xn_scr", [LQ, D], BF16, kind=dk)
    ps_scr = nc.dram_tensor("ps_scr", [LQ, D], BF16)
    rs_scr = nc.dram_tensor("rs_scr", [TOK, D], BF16)
    xq_scr = nc.dram_tensor("xq_scr", [TOK, D], BF16, kind=dk)
    wg_scr = nc.dram_tensor("wg_scr", [DI, D], BF16, kind=dk)
    wu_scr = nc.dram_tensor("wu_scr", [DI, D], BF16, kind=dk)
    wd_scr = nc.dram_tensor("wd_scr", [D, DI], BF16, kind=dk)
    hh_scr = nc.dram_tensor("hh_scr", [DI, TOK], BF16, kind=dk)
    y_scr = nc.dram_tensor("y_scr", [TOK, D], BF16, kind=dk)
    mx1_i = nc.dram_tensor("mx1_i", [1, 128], F32)
    mx1_o = nc.dram_tensor("mx1_o", [1, 128], F32)
    mx2_i = nc.dram_tensor("mx2_i", [1, 128], F32)
    mx2_o = nc.dram_tensor("mx2_o", [1, 128], F32)
    sc_scr = nc.dram_tensor("sc_scr", [1, 8], F32, kind=dk)

    es = contextlib.ExitStack()
    with es:
        def sb(name, shape, dt=F32):
            return es.enter_context(nc.sbuf_tensor(name, shape, dt))

        def ch(name):
            return Ch(es, nc, name)

        biase = sb("biase", [128, 1], F32)
        biasD2e = sb("biasD2e", [128, 1], F32)
        biaslnD = sb("biaslnD", [128, 1], F32)
        biasM = sb("biasM", [128, 1], F32)
        bp_s = sb("bp_s", [128, 12], F32)
        ssq_s = sb("ssq_s", [128, 12], F32)
        mraw_s = sb("mraw_s", [128, 12], F32)
        m_s = sb("m_s", [128, 12], F32)
        m2_s = sb("m2_s", [128, 12], F32)
        var_s = sb("var_s", [128, 12], F32)
        rstd_s = sb("rstd_s", [128, 12], F32)
        lnv_s = sb("lnv_s", [128, 12], F32)
        mx1 = sb("mx1", [128, 1], F32)
        mx2 = sb("mx2", [128, 1], F32)
        mxb4 = sb("mxb4", [128, 4], F32)
        mxbb = sb("mxbb", [128, 1], F32)
        mxb22 = sb("mxb22", [128, NIC], F32)
        mxrow = sb("mxrow", [1, 128], F32)
        mxrow2 = sb("mxrow2", [1, 128], F32)
        scl_x = sb("scl_x", [1, 1], F32)
        scl_d = sb("scl_d", [1, 1], F32)
        e1t = sb("e1t", [1, 1], F32)
        e1t2 = sb("e1t2", [1, 1], F32)
        gxt = sb("gxt", [1, 1], F32)
        gx2t = sb("gx2t", [1, 1], F32)
        t11a = sb("t11a", [1, 1], F32)
        t11b = sb("t11b", [1, 1], F32)
        s1t = sb("s1t", [1, 1], F32)
        is1t = sb("is1t", [1, 1], F32)
        s2t = sb("s2t", [1, 1], F32)
        is2t = sb("is2t", [1, 1], F32)
        bc_s1 = sb("bc_s1", [128, 1], F32)
        bc_is1 = sb("bc_is1", [128, 1], F32)
        bc_s2 = sb("bc_s2", [128, 1], F32)
        ssqF = sb("ssqF", [128, 8], F32)
        mrawF = sb("mrawF", [128, 8], F32)
        m2F = sb("m2F", [128, 8], F32)
        varF = sb("varF", [128, 8], F32)
        lnvF = sb("lnvF", [128, 8], F32)
        mxq4 = sb("mxq4", [128, 4], F32)
        mxb8 = sb("mxb8", [128, 8], F32)
        junk = sb("junk", [128, D], BF16)
        xo_h = [sb(f"xo_h{i}", [128, 512], F32) for i in range(2)]
        rs_h = [sb(f"rs_h{i}", [128, 512], BF16) for i in range(2)]
        yst = [sb(f"yst{i}", [128, D], BF16) for i in range(8)]
        xnb1 = sb("xnb1", [128, D], BF16)
        m_arr = sb("m_arr", [128, NOB], F32)
        rstd2_arr = sb("rstd2_arr", [128, NOB], F32)
        rs2_arr = sb("rs2_arr", [128, NOB], F32)
        rmx = sb("rmx", [128, 4], F32)
        rcp = sb("rcp", [128, 4], F32)
        sdd = sb("sdd", [128, 4], F32)
        swg_arr = sb("swg_arr", [128, NIC], F32)
        swu_arr = sb("swu_arr", [128, NIC], F32)
        sgs1 = sb("sgs1", [128, NIC], F32)
        sus1 = sb("sus1", [128, NIC], F32)
        bc_is2 = sb("bc_is2", [128, 1], F32)

        def chp(name, per=16, nring=2):
            return ChP(es, nc, name, per=per, nring=nring)

        s_x = chp("s_x", nring=3); s_xnw = chp("s_xnw"); s_xnr = chp("s_xnr", per=16 * ND)
        s_pw = chp("s_pw")
        s_fw = chp("s_fw", nring=4); s_fww = chp("s_fww", nring=4)
        s_rs = chp("s_rs"); s_xo = chp("s_xo")
        s_xqw = chp("s_xqw"); s_xqr = chp("s_xqr", per=16 * NHC)
        s_hw = chp("s_hw"); s_out = chp("s_out")
        s_hr = chp("s_hr", per=16 * NIC)
        s_clip = ch("s_clip"); s_aw = chp("s_aw", nring=4)
        s_fwr = ch("s_fwr"); s_wdr = ch("s_wdr")
        s_cc = ch("s_cc"); s_mx = ch("s_mx")
        s_gx1 = ch("s_gx1"); s_gx2 = ch("s_gx2")
        s_sc = ch("s_sc")

        c_init = ch("c_init"); c_sq = ch("c_sq"); c_var = ch("c_var")
        c_rstd = ch("c_rstd"); c_xn = ch("c_xn"); c_wcast = ch("c_wcast")
        c_pk = ch("c_pk"); c_pv = ch("c_pv"); c_pq = ch("c_pq")
        c_k = ch("c_k"); c_e = ch("c_e"); c_kvf = ch("c_kvf")
        c_kfree = ch("c_kfree"); c_epk = ch("c_epk"); c_d = ch("c_d")
        c_O = ch("c_O"); c_wc = ch("c_wc"); c_pev = ch("c_pev")
        c_teT = ch("c_teT"); c_wq = ch("c_wq"); c_y = ch("c_y")
        c_f1 = ch("c_f1"); c_e1 = ch("c_e1"); c_s1 = ch("c_s1")
        c_xq = ch("c_xq"); c_pg = ch("c_pg"); c_gu = ch("c_gu")
        c_sg = ch("c_sg"); c_hh = ch("c_hh"); c_mx2 = ch("c_mx2")
        c_s2 = ch("c_s2"); c_h = ch("c_h"); c_dn = ch("c_dn")
        c_oev = ch("c_oev"); c_bp = ch("c_bp"); c_rcp = ch("c_rcp")
        c_wt1 = ch("c_wt1")
        c_yF = ch("c_yF"); c_sqF = ch("c_sqF"); c_varF = ch("c_varF")
        c_rstdF = ch("c_rstdF"); c_mxF = ch("c_mxF")
        s_yw = chp("s_yw"); s_yq = chp("s_yq"); s_yo = chp("s_yo")

        def f1_g(g, t):
            for hf in range(2):
                i2 = 2 * t + hf
                s_xo.inc(g.dma_start(
                    out=xo_h[i2 % 2][:],
                    in_=x_own.ap()[t * 128:(t + 1) * 128,
                                   hf * 512:(hf + 1) * 512]), i2)
            s_cc.wait(g, CIDX[t] + 1)
            for hf in range(2):
                i2 = 2 * t + hf
                s_rs.inc(g.dma_start(
                    out=rs_h[i2 % 2][:],
                    in_=rs_scr.ap()[t * 128:(t + 1) * 128,
                                    hf * 512:(hf + 1) * 512]), i2)
            s_yw.wait_item(g, t - 8)
            if t >= 8:
                c_sqF.wait(g, t - 7)
                c_mxF.wait(g, t - 7)
            ins = None
            for hf in range(2):
                i2 = 2 * t + hf
                s_xo.wait_item(g, i2)
                s_rs.wait_item(g, i2)
                ins = g.tensor_add(yst[t % 8][:, hf * 512:(hf + 1) * 512],
                                   xo_h[i2 % 2][:], rs_h[i2 % 2][:])
            c_yF.inc(ins, 1)
            s_yw.inc(g.dma_start(out=y_scr.ap()[t * 128:(t + 1) * 128, :],
                                 in_=yst[t % 8][:]), t)

        def f1_sa(s, u):
            grF = (u % 2) * 4
            c_varF.wait(s, u - 1)
            for n in range(4):
                t = 4 * u + n
                c_yF.wait(s, t + 1)
                s.activation(junk[:], yst[t % 8][:], AF.Square, bias=0.0,
                             scale=1.0, accum_out=ssqF[:, grF + n:grF + n + 1])
                c_sqF.inc(s.activation(
                    junk[:], yst[t % 8][:], AF.Identity, bias=0.0, scale=1.0,
                    accum_out=mrawF[:, grF + n:grF + n + 1]), 1)

        def f1_sb(s, u):
            grF = (u % 2) * 4
            c_varF.wait(s, u + 1)
            s.activation(lnvF[:, grF:grF + 4], varF[:, grF:grF + 4], AF.Ln,
                         bias=biase[:], scale=1.0)
            s.drain()
            c_rstdF.inc(s.activation(
                rstd2_arr[:, 4 * u:4 * u + 4], lnvF[:, grF:grF + 4],
                AF.Exp, bias=0.0, scale=-0.5), 1)

        def f1_da(v, u):
            grF = (u % 2) * 4
            c_sqF.wait(v, 4 * (u + 1))
            v.tensor_scalar(m_arr[:, 4 * u:4 * u + 4], mrawF[:, grF:grF + 4],
                            1.0 / D, None, ALU.mult)
            c_rstdF.wait(v, u - 1)
            v.drain()
            v.tensor_mul(m2F[:, grF:grF + 4], m_arr[:, 4 * u:4 * u + 4],
                         m_arr[:, 4 * u:4 * u + 4])
            v.drain()
            c_varF.inc(v.scalar_tensor_tensor(
                varF[:, grF:grF + 4], ssqF[:, grF:grF + 4], 1.0 / D,
                m2F[:, grF:grF + 4], ALU.mult, ALU.subtract), 1)
            v.drain()
            for n in range(4):
                t = 4 * u + n
                v.tensor_scalar(xnb1[:], yst[t % 8][:], m_arr[:, t:t + 1],
                                None, ALU.subtract)
                c_mxF.inc(v.reduce_max(mxb8[:, grF + n:grF + n + 1], xnb1[:],
                                       AX.X, apply_absolute_value=True), 1)

        def f1_db(v, u):
            grF = (u % 2) * 4
            c_rstdF.wait(v, u + 1)
            v.drain()
            v.tensor_mul(mxq4[:], mxb8[:, grF:grF + 4],
                         rstd2_arr[:, 4 * u:4 * u + 4])
            v.drain()
            v.reduce_max(mxbb[:], mxq4[:], AX.X)
            v.drain()
            ins = v.tensor_max(mx1[:], mx1[:], mxbb[:])
            if u == NQF - 1:
                c_f1.inc(ins, 1)

        if NTT == 16:
            ACT_MAP = {12: [("sa", 0)], 13: [("sa", 1), ("sb", 0)],
                       14: [("sa", 2), ("sb", 1)], 15: [("sa", 3), ("sb", 2)]}
            DVE_MAP = {13: [("da", 0)], 14: [("da", 1), ("db", 0)],
                       15: [("da", 2), ("db", 1)]}
            ACT_POST = [("sa", 4), ("sb", 3), ("sa", 5), ("sb", 4),
                        ("sa", 6), ("sb", 5), ("sa", 7), ("sb", 6), ("sb", 7)]
            DVE_POST = [("da", 3), ("db", 2), ("da", 4), ("db", 3),
                        ("da", 5), ("db", 4), ("da", 6), ("db", 5),
                        ("da", 7), ("db", 6), ("db", 7)]
        else:
            ACT_MAP = {}
            DVE_MAP = {}
            ACT_POST = []
            DVE_POST = []
            for u in range(NQF):
                ACT_POST += [("sa", u), ("sb", u)]
                DVE_POST += [("da", u), ("db", u)]

        def f1_act(s, items):
            for kind, u in items:
                (f1_sa if kind == "sa" else f1_sb)(s, u)

        def f1_dve(v, items):
            for kind, u in items:
                (f1_da if kind == "da" else f1_db)(v, u)

        block = es.enter_context(nc.Block())

        sA = contextlib.ExitStack()
        with sA:
            def sba(name, shape, dt=F32):
                return sA.enter_context(nc.sbuf_tensor(name, shape, dt))

            ones_bf = sba("ones_bf", [128, 512], BF16)
            x_t = [sba(f"x_t{i}", [128, 4 * D]) for i in range(3)]
            xnst = [sba(f"xnst{i}", [128, 4 * D], BF16) for i in range(2)]
            xnf = [sba(f"xnf{i}", [128, ND * 512], BF16) for i in range(2)]
            wk_sb = sba("wk_sb", [128, ND * EHW], BF16)
            wv_sb = sba("wv_sb", [128, ND * EHW], BF16)
            wq_sb = sba("wq_sb", [128, ND * EHW], BF16)
            wc_sb = sba("wc_sb", [128, NEC * D], BF16)
            K2 = [sba(f"K2_{i}", [128, 512], BF16) for i in range(2)]
            e2 = [sba(f"e2_{i}", [128, 512], BF16) for i in range(2)]
            KVm = sba("KVm", [128, 512], BF16)
            cumK = sba("cumK", [128, NEC * 512])
            cumKV = sba("cumKV", [128, NEC * 512])
            epk = [sba(f"epk{i}", [128, 512], BF16) for i in range(2)]
            dln = sba("dln", [128, 512])
            dex = [sba(f"dex{i}", [128, 512], BF16) for i in range(2)]
            Obuf = [sba(f"Obuf{i}", [128, NEC * 512], BF16) for i in range(2)]
            pst = [sba(f"pst{i}", [128, D], BF16) for i in range(2)]
            wld = [sba(f"wld{i}", [128, D]) for i in range(4)]
            wqst = [sba(f"wqst{i}", [128, D], BF16) for i in range(4)]

            pk = [sA.enter_context(nc.psum_tensor(f"pk{i}", [128, 512], F32)) for i in range(2)]
            pv = [sA.enter_context(nc.psum_tensor(f"pv{i}", [128, 512], F32)) for i in range(2)]
            pq = [sA.enter_context(nc.psum_tensor(f"pq{i}", [128, 512], F32)) for i in range(2)]
            pwc = [sA.enter_context(nc.psum_tensor(f"pwc{i}", [128, 512], F32)) for i in range(2)]

            AWS = ((wq_d, wq_sb, ND, EHW),
                   (wk_d, wk_sb, ND, EHW),
                   (wv_d, wv_sb, ND, EHW),
                   (wc_d, wc_sb, NEC, D))

            PREP = ([(wg_d, wg_scr, D, "g", ic) for ic in range(NIC)]
                    + [(wu_d, wu_scr, D, "u", ic) for ic in range(NIC)])
            NGU = len(PREP)
            NPQ = (NGU + 3) // 4

            @block.gpsimd
            def _(g):
                s_clip.inc(g.dma_start(out=scl_x[:], in_=clipx_d.ap()), 16)
                s_clip.inc(g.dma_start(out=scl_d[:], in_=clipd_d.ap()), 16)
                for wi, (wdram, _sb, nch, width) in enumerate(AWS):
                    nq_ = nch // 4
                    for qt in range(4):
                        c_wcast.wait(g, 4 * wi + qt - 3)
                        rs_ = wdram.ap()[qt * nq_ * 128:(qt + 1) * nq_ * 128, :]
                        s_aw.inc(g.dma_start(
                            out=sb3(wld[qt][:, 0:nq_ * width], nq_),
                            in_=rows3(rs_)), 4 * wi + qt)

                def rs_chunk(c):
                    sps, spe = SPANS[c]
                    s_pw.wait_upto(g, spe // 128 - 1)
                    s_cc.inc(g.collective_compute(
                        "ReduceScatter", ALU.add, replica_groups=PAIRS,
                        ins=[ps_scr.ap()[sps:spe, :].opt()],
                        outs=[rs_scr.ap()[sps // 2:spe // 2, :].opt()]), 1)

                def prep_load(j):
                    wdram, _scr, cols, kind, row = PREP[j]
                    if j >= 4:
                        c_wq.wait(g, j - 3)
                    else:
                        c_wcast.wait(g, 16)
                    s_fw.inc(g.dma_start(
                        out=wld[j % 4][:, 0:cols],
                        in_=wdram.ap()[row * 128:(row + 1) * 128, :]), j)

                if NTT == 16 and NGU == 44 and NOB == 32:
                    SEQ = ([("j", j) for j in range(16)] + [("c", 0)]
                           + [("j", j) for j in range(16, 44)]
                           + [("f", t) for t in range(0, 8)] + [("c", 1)]
                           + [("f", t) for t in range(8, 12)] + [("c", 2)]
                           + [("f", t) for t in range(12, 20)] + [("c", 3)]
                           + [("f", t) for t in range(20, 24)] + [("c", 4)]
                           + [("f", t) for t in range(24, 32)])
                else:
                    SEQ = ([("j", j) for j in range(NGU)]
                           + [("c", c) for c in range(NCH)]
                           + [("f", t) for t in range(NOB)])
                for kind_, i_ in SEQ:
                    if kind_ == "j":
                        prep_load(i_)
                    elif kind_ == "c":
                        rs_chunk(i_)
                    else:
                        f1_g(g, i_)

            @block.vector
            def _(v):
                v.memset(ones_bf[:], 1.0)
                v.memset(biase[:], EPS_LN)
                v.memset(biasD2e[:], float(D) * float(D) * EPS_LN)
                v.memset(biaslnD[:], float(np.log(D)))
                v.memset(biasM[:], MAGIC)
                v.memset(mx1[:], 0.0)
                v.memset(mx2[:], 0.0)
                v.memset(mxrow[:], 0.0)
                c_init.inc(v.memset(mxrow2[:], 0.0), 1)
                for wi, (_w, wsb, nch, width) in enumerate(AWS):
                    qw_ = (nch // 4) * width
                    for qt in range(4):
                        s_aw.wait_item(v, 4 * wi + qt)
                        c_wcast.inc(v.tensor_copy(
                            wsb[:, qt * qw_:(qt + 1) * qw_],
                            wld[qt][:, 0:qw_]), 1)

                def prep_quad(qd):
                    tiles = list(range(4 * qd, min(4 * qd + 4, NGU)))
                    for j in tiles:
                        s_fw.wait_item(v, j)
                        s_fww.wait_item(v, j - 4)
                        v.reduce_max(rmx[:, j % 4:j % 4 + 1],
                                     wld[j % 4][:, 0:PREP[j][2]], AX.X,
                                     apply_absolute_value=True)
                    v.drain()
                    for j in tiles:
                        _w, _scr, cols, kind, row = PREP[j]
                        sd_ = swg_arr[:, row:row + 1] if kind == "g" else swu_arr[:, row:row + 1]
                        v.tensor_scalar(sd_, rmx[:, j % 4:j % 4 + 1],
                                        1.0 / 127.0, None, ALU.mult)
                    v.drain()
                    for j in tiles:
                        _w, _scr, cols, kind, row = PREP[j]
                        sd_ = swg_arr[:, row:row + 1] if kind == "g" else swu_arr[:, row:row + 1]
                        ins_ = v.reciprocal(rcp[:, j % 4:j % 4 + 1], sd_)
                    c_rcp.inc(ins_, 1)
                    for j in tiles:
                        cols_ = PREP[j][2]
                        c_wt1.wait(v, j + 1)
                        c_wq.inc(v.tensor_scalar(
                            wqst[j % 4][:, 0:cols_], wld[j % 4][:, 0:cols_],
                            MAGIC, None, ALU.subtract), 1)

                def stats_xn(tt):
                    gr = (tt % 3) * 4
                    s_x.wait_item(v, tt)
                    s_xnw.wait_item(v, tt - 2)
                    c_sq.wait(v, 4 * (tt + 1))
                    v.tensor_mul(m2_s[:, gr:gr + 4], mraw_s[:, gr:gr + 4],
                                 mraw_s[:, gr:gr + 4])
                    c_rstd.wait(v, tt - 2)
                    v.drain()
                    c_var.inc(v.scalar_tensor_tensor(
                        var_s[:, gr:gr + 4], ssq_s[:, gr:gr + 4], float(D),
                        m2_s[:, gr:gr + 4], ALU.mult, ALU.subtract), 1)
                    c_rstd.wait(v, tt + 1)
                    c_bp.inc(v.scalar_tensor_tensor(
                        bp_s[:, gr:gr + 4], mraw_s[:, gr:gr + 4], -1.0 / D,
                        rstd_s[:, gr:gr + 4], ALU.mult, ALU.mult), 1)
                    v.drain()
                    for n in range(4):
                        c_xn.inc(v.tensor_scalar(
                            xnst[tt % 2][:, n * D:(n + 1) * D],
                            x_t[tt % 3][:, n * D:(n + 1) * D],
                            rstd_s[:, gr + n:gr + n + 1],
                            bp_s[:, gr + n:gr + n + 1],
                            ALU.mult, ALU.add), 1)

                def emit_O(g):
                    tt_, ec_ = g // NEC, g % NEC
                    c_d.wait(v, g + 1)
                    if ec_ == 0 and tt_ >= 2:
                        c_wc.wait(v, 8 * (tt_ - 1))
                    c_O.inc(v.tensor_mul(
                        Obuf[tt_ % 2][:, ec_ * 512:(ec_ + 1) * 512],
                        cumKV[:, ec_ * 512:(ec_ + 1) * 512], dex[g % 2][:]), 1)

                def chain(tt):
                    for ec in range(NEC):
                        g = NEC * tt + ec
                        par = g % 2
                        ck = cumK[:, ec * 512:(ec + 1) * 512]
                        ckv = cumKV[:, ec * 512:(ec + 1) * 512]
                        c_k.wait(v, g + 1)
                        c_pv.wait(v, g + 1)
                        c_kvf.inc(v.tensor_mul(KVm[:], K2[par][:], pv[par][:]), 1)
                        initK = EPS_ATTN if tt == 0 else ck[:, 511:512]
                        c_kfree.inc(v.tensor_tensor_scan(
                            ck, ones_bf[:], K2[par][:], initK, ALU.mult, ALU.add), 1)
                        initV = 0.0 if tt == 0 else ckv[:, 511:512]
                        v.tensor_tensor_scan(ckv, ones_bf[:], KVm[:], initV,
                                             ALU.mult, ALU.add)
                        c_e.wait(v, g + 1)
                        c_epk.inc(v.scalar_tensor_tensor(
                            epk[par][:], e2[par][:], 1.0, ck,
                            ALU.add, ALU.mult), 1)
                        if g >= 1:
                            emit_O(g - 1)

                stats_xn(0)
                if NTT > 1:
                    stats_xn(1)
                for tt in range(NTT):
                    if tt + 2 < NTT:
                        stats_xn(tt + 2)
                    chain(tt)
                    f1_dve(v, DVE_MAP.get(tt, []))
                    if tt < NPQ:
                        prep_quad(tt)
                emit_O(NEC * NTT - 1)
                for qd in range(NTT, NPQ):
                    prep_quad(qd)

            @block.scalar
            def _(s):
                s_clip.wait(s, 32)
                c_e1.inc(s.activation(e1t[:], scl_x[:], AF.Exp, bias=0.0, scale=-1.0), 1)
                c_e1.inc(s.activation(e1t2[:], scl_d[:], AF.Exp, bias=0.0, scale=-1.0), 1)
                c_init.wait(s, 1)

                def stats_act(tt):
                    gr = (tt % 3) * 4
                    s_x.wait_item(s, tt)
                    c_var.wait(s, tt - 2)
                    c_bp.wait(s, tt - 2)
                    for n in range(4):
                        s.activation(junk[:], x_t[tt % 3][:, n * D:(n + 1) * D],
                                     AF.Square, bias=0.0, scale=1.0,
                                     accum_out=ssq_s[:, gr + n:gr + n + 1])
                        c_sq.inc(s.activation(junk[:], x_t[tt % 3][:, n * D:(n + 1) * D],
                                              AF.Identity, bias=0.0, scale=1.0,
                                              accum_out=mraw_s[:, gr + n:gr + n + 1]), 1)
                    c_var.wait(s, tt + 1)
                    s.activation(lnv_s[:, gr:gr + 4], var_s[:, gr:gr + 4],
                                 AF.Ln, bias=biasD2e[:], scale=1.0)
                    s.drain()
                    c_rstd.inc(s.activation(rstd_s[:, gr:gr + 4], lnv_s[:, gr:gr + 4],
                                            AF.Exp, bias=biaslnD[:], scale=-0.5), 1)


                def emit_lnd(g):
                    c_epk.wait(s, g + 1)
                    s.activation(dln[:], epk[g % 2][:], AF.Ln, bias=0.0, scale=1.0)
                    if g >= 2:
                        c_O.wait(s, g - 1)
                    c_d.inc(s.activation(dex[g % 2][:], dln[:], AF.Exp,
                                         bias=0.0, scale=-1.0), 1)

                def chain_act(tt):
                    for ec in range(NEC):
                        g = NEC * tt + ec
                        par = g % 2
                        c_pk.wait(s, g + 1)
                        if g >= 2:
                            c_kfree.wait(s, g - 1)
                        c_k.inc(s.activation(K2[par][:], pk[par][:], AF.Exp,
                                             bias=0.0, scale=1.0), 1)
                        c_pq.wait(s, g + 1)
                        if g >= 2:
                            c_epk.wait(s, g - 1)
                        c_e.inc(s.activation(e2[par][:], pq[par][:], AF.Exp,
                                             bias=0.0, scale=-1.0), 1)
                        if g >= 1:
                            emit_lnd(g - 1)

                def evac_pdma(tt):
                    def dma_slice(slc):
                        q = 4 * tt + slc
                        t0 = q * 128
                        row = ps2_row(t0)
                        s_pw.inc(s.dma_start(
                            out=ps_scr.ap()[row:row + 128, :],
                            in_=pst[q % 2][:]), q)

                    for slc in range(4):
                        q = 4 * tt + slc
                        s_pw.wait_item(s, q - 2)
                        for hf in range(2):
                            idx = 8 * tt + 2 * slc + hf
                            c_wc.wait(s, idx + 1)
                            c_pev.inc(s.copy(pst[q % 2][:, hf * 512:(hf + 1) * 512],
                                             pwc[idx % 2][:]), 1)
                        if slc >= 1:
                            dma_slice(slc - 1)
                    dma_slice(3)

                def prep_act(qd):
                    c_rcp.wait(s, qd + 1)
                    for j in range(4 * qd, min(4 * qd + 4, NGU)):
                        _w, _scr, cols, kind, row = PREP[j]
                        c_wt1.inc(s.activation(
                            wld[j % 4][:, 0:cols], wld[j % 4][:, 0:cols],
                            AF.Identity, bias=biasM[:],
                            scale=rcp[:, j % 4:j % 4 + 1]), 1)

                stats_act(0)
                if NTT > 1:
                    stats_act(1)
                for tt in range(NTT):
                    if tt + 2 < NTT:
                        stats_act(tt + 2)
                    chain_act(tt)
                    if tt >= 1:
                        evac_pdma(tt - 1)
                    f1_act(s, ACT_MAP.get(tt, []))
                    if tt < NPQ:
                        prep_act(tt)
                emit_lnd(NEC * NTT - 1)
                evac_pdma(NTT - 1)
                for qd in range(NTT, NPQ):
                    prep_act(qd)

            @block.sync
            def _(sy):
                for tt0 in range(min(3, NTT)):
                    src = rows3(xb.ap()[tt0 * 512:(tt0 + 1) * 512, :])
                    s_x.inc(sy.dma_start(out=sb3(x_t[tt0 % 3][:], 4), in_=src), tt0)
                for tt in range(NTT):
                    c_xn.wait(sy, 4 * (tt + 1))
                    s_xnw.inc(sy.dma_start(
                        out=rows3(xn_scr.ap()[tt * 512:(tt + 1) * 512, :]),
                        in_=sb3(xnst[tt % 2][:], 4)), tt)
                    s_xnw.wait_item(sy, tt)
                    c_teT.wait(sy, tt - 2)
                    for dc in range(ND):
                        s_xnr.inc(sy.dma_start(
                            out=xnf[tt % 2][:, dc * 512:(dc + 1) * 512],
                            in_=xn_scr.ap()[tt * 512:(tt + 1) * 512,
                                            dc * 128:(dc + 1) * 128],
                            transpose=True), tt)
                    if tt + 3 < NTT:
                        c_xn.wait(sy, 4 * (tt + 1))
                        c_sq.wait(sy, 4 * (tt + 1))
                        src = rows3(xb.ap()[(tt + 3) * 512:(tt + 4) * 512, :])
                        s_x.inc(sy.dma_start(out=sb3(x_t[tt % 3][:], 4), in_=src),
                                tt + 3)
                    if 1 <= tt and tt - 1 < NPQ:
                        for j in range(4 * (tt - 1), min(4 * tt, NGU)):
                            _w, scr, cols, kind, row = PREP[j]
                            c_wq.wait(sy, j + 1)
                            s_fww.inc(sy.dma_start(
                                out=scr.ap()[row * 128:(row + 1) * 128, :],
                                in_=wqst[j % 4][:, 0:cols]), j)
                for qd in range(min(NTT - 1, NPQ), NPQ):
                    for j in range(4 * qd, min(4 * qd + 4, NGU)):
                        _w, scr, cols, kind, row = PREP[j]
                        c_wq.wait(sy, j + 1)
                        s_fww.inc(sy.dma_start(
                            out=scr.ap()[row * 128:(row + 1) * 128, :],
                            in_=wqst[j % 4][:, 0:cols]), j)

            @block.tensor
            def _(te):
                c_wcast.wait(te, 4)

                def proj(tt):
                    s_xnr.wait_item(te, tt)
                    for ec in range(NEC):
                        g = NEC * tt + ec
                        par = g % 2
                        if g >= 2:
                            c_k.wait(te, g - 1)
                            c_kvf.wait(te, g - 1)
                            c_e.wait(te, g - 1)
                        for wsb, psm, chn in ((wk_sb, pk, c_pk), (wv_sb, pv, c_pv),
                                              (wq_sb, pq, c_pq)):
                            for dc in range(ND):
                                last = te.matmul(
                                    psm[par][:],
                                    wsb[:, dc * EHW + ec * 128: dc * EHW + (ec + 1) * 128],
                                    xnf[tt % 2][:, dc * 512:(dc + 1) * 512],
                                    start=(dc == 0), stop=(dc == ND - 1))
                            chn.inc(last, 1)

                def wc(tt):
                    c_O.wait(te, NEC * (tt + 1))
                    first_wc = True
                    for slc in range(4):
                        for hf in range(2):
                            idx = 8 * tt + 2 * slc + hf
                            if idx >= 2:
                                c_pev.wait(te, idx - 1)
                            for ec in range(NEC):
                                last = te.matmul(
                                    pwc[idx % 2][:],
                                    Obuf[tt % 2][:, ec * 512 + slc * 128:
                                                  ec * 512 + (slc + 1) * 128],
                                    wc_sb[:, ec * D + hf * 512: ec * D + (hf + 1) * 512],
                                    start=(ec == 0), stop=(ec == NEC - 1))
                                if first_wc and ec == 0:
                                    c_teT.inc(last, 1)
                                    first_wc = False
                            c_wc.inc(last, 1)

                for tt in range(NTT):
                    proj(tt)
                    if tt >= 1:
                        wc(tt - 1)
                wc(NTT - 1)

        sFF = contextlib.ExitStack()
        with sFF:
            def sbf(name, shape, dt=F32):
                return sFF.enter_context(nc.sbuf_tensor(name, shape, dt))


            if ACT_POST:
                @block.scalar
                def _(s):
                    f1_act(s, ACT_POST)
            if DVE_POST:
                @block.vector
                def _(v):
                    f1_dve(v, DVE_POST)

            sPD = contextlib.ExitStack()
            with sPD:
                wldd = [sPD.enter_context(nc.sbuf_tensor(f"wldd{i}", [128, DI], F32))
                        for i in range(2)]
                wqsd = [sPD.enter_context(nc.sbuf_tensor(f"wqsd{i}", [128, DI], BF16))
                        for i in range(2)]
                wtmpd = sPD.enter_context(nc.sbuf_tensor("wtmpd", [128, DI], F32))

                @block.gpsimd
                def _(g):
                    for hc in range(NHC):
                        if hc >= 2:
                            c_wq.wait(g, NGU + hc - 1)
                        else:
                            s_pw.wait_upto(g, 4 * NTT - 1)
                            c_pev.wait(g, 8 * NTT)
                            c_wc.wait(g, 8 * NTT)
                        s_fw.inc(g.dma_start(
                            out=wldd[hc % 2][:],
                            in_=wd_d.ap()[hc * 128:(hc + 1) * 128, :]), NGU + hc)

                @block.vector
                def _(v):
                    for p2 in range(NHC // 2):
                        hcs = [2 * p2, 2 * p2 + 1]
                        for hc in hcs:
                            s_fw.wait_item(v, NGU + hc)
                            s_fww.wait_item(v, NGU + hc - 2)
                            v.reduce_max(rmx[:, hc % 2:hc % 2 + 1] if False else sdd[:, hc % 2:hc % 2 + 1],
                                         wldd[hc % 2][:], AX.X,
                                         apply_absolute_value=True)
                        v.drain()
                        for hc in hcs:
                            v.tensor_scalar(sdd[:, 2 + hc % 2:3 + hc % 2],
                                            sdd[:, hc % 2:hc % 2 + 1],
                                            1.0 / 127.0, None, ALU.mult)
                        v.drain()
                        for hc in hcs:
                            v.reciprocal(rcp[:, hc % 2:hc % 2 + 1],
                                         sdd[:, 2 + hc % 2:3 + hc % 2])
                        v.drain()
                        for hc in hcs:
                            v.tensor_scalar(wtmpd[:], wldd[hc % 2][:],
                                            rcp[:, hc % 2:hc % 2 + 1], MAGIC,
                                            ALU.mult, ALU.add)
                            v.tensor_scalar(wtmpd[:], wtmpd[:],
                                            MAGIC, None, ALU.subtract)
                            c_wq.inc(v.tensor_scalar(
                                wqsd[hc % 2][:], wtmpd[:],
                                sdd[:, 2 + hc % 2:3 + hc % 2], None, ALU.mult), 1)

                @block.scalar
                def _(s):
                    for hc in range(NHC):
                        c_wq.wait(s, NGU + hc + 1)
                        s_fww.inc(s.dma_start(
                            out=wd_scr.ap()[hc * 128:(hc + 1) * 128, :],
                            in_=wqsd[hc % 2][:]), NGU + hc)


            sWG = contextlib.ExitStack()
            with sWG:
                wgT = sWG.enter_context(nc.sbuf_tensor("wgT", [128, NHC * DI], BF16))
                wuT = sWG.enter_context(nc.sbuf_tensor("wuT", [128, NHC * DI], BF16))

                @block.sync
                def _(sy):
                    s_pw.wait_upto(sy, 4 * NTT - 1)
                    c_pev.wait(sy, 8 * NTT)
                    c_wc.wait(sy, 8 * NTT)
                    s_fww.wait_upto(sy, NW - 1)
                    for wi, (scr, dst) in enumerate(((wg_scr, wgT), (wu_scr, wuT))):
                        for hc in range(NHC):
                            s_fwr.inc(sy.dma_start(
                                out=dst[:, hc * DI:(hc + 1) * DI],
                                in_=scr.ap()[0:DI, hc * 128:(hc + 1) * 128],
                                transpose=True), 16)

                sF1 = contextlib.ExitStack()
                with sF1:
                    @block.gpsimd
                    def _(g):
                        c_f1.wait(g, 1)
                        s_mx.inc(g.dma_start(out=mx1_i.ap(), in_=mx1[:]), 16)
                        g.wait_ge(s_mx.sem, 16)
                        s_cc.inc(g.collective_compute(
                            "AllReduce", ALU.max, replica_groups=ALL8,
                            ins=[mx1_i.ap().opt()], outs=[mx1_o.ap().opt()]), 1)

                    @block.sync
                    def _(sy):
                        s_cc.wait(sy, NCH + 1)
                        s_gx1.inc(sy.dma_start(out=mxrow[:], in_=mx1_o.ap()), 16)
                        s_sc.wait(sy, 32)
                        s_gx1.inc(sy.dma_start(out=bc_s1[:], in_=bcast_ap(sc_scr, 0)), 16)
                        s_gx1.inc(sy.dma_start(out=bc_is1[:], in_=bcast_ap(sc_scr, 1)), 16)

                    @block.vector
                    def _(v):
                        c_e1.wait(v, 1)
                        s_gx1.wait(v, 16)
                        v.reduce_max(gxt[:], mxrow[0:1, :], AX.X)
                        v.drain()
                        v.tensor_scalar(t11a[:], e1t[:], 1.0, None, ALU.add)
                        v.drain()
                        v.reciprocal(t11b[:], t11a[:])
                        v.drain()
                        v.tensor_scalar(s1t[:], t11b[:], gxt[0:1, 0:1], 1.0 / 127.0,
                                        ALU.mult, ALU.mult)
                        v.drain()
                        c_s1.inc(v.reciprocal(is1t[:], s1t[:]), 1)
                        s_gx1.wait(v, 48)
                        v.tensor_scalar(rs2_arr[:], rstd2_arr[:], bc_is1[:, 0:1],
                                        None, ALU.mult)
                        v.tensor_scalar(sgs1[:], swg_arr[:], bc_s1[:, 0:1],
                                        None, ALU.mult)
                        c_s1.inc(v.tensor_scalar(sus1[:], swu_arr[:], bc_s1[:, 0:1],
                                                 None, ALU.mult), 1)
                        v.drain()

                    @block.scalar
                    def _(s):
                        c_s1.wait(s, 1)
                        s_sc.inc(s.dma_start(out=sc_scr.ap()[0:1, 0:1], in_=s1t[:]), 16)
                        s_sc.inc(s.dma_start(out=sc_scr.ap()[0:1, 1:2], in_=is1t[:]), 16)

                sF2 = contextlib.ExitStack()
                with sF2:
                    xqst = [sF2.enter_context(nc.sbuf_tensor(f"xqst{i}", [128, 4 * D], BF16))
                            for i in range(2)]
                    yq = [sF2.enter_context(nc.sbuf_tensor(f"yq{i}", [128, 4 * D], BF16))
                          for i in range(2)]
                    xqf = [sF2.enter_context(nc.sbuf_tensor(f"xqf{i}", [128, NHC * 512], BF16))
                           for i in range(2)]
                    sgb = [sF2.enter_context(nc.sbuf_tensor(f"sgb{i}", [128, 512], BF16))
                           for i in range(2)]
                    hhst = [sF2.enter_context(nc.sbuf_tensor(f"hhst{i}", [128, 512], BF16))
                            for i in range(2)]
                    xnb2 = sF2.enter_context(nc.sbuf_tensor("xnb2", [128, D], F32))
                    xnb3 = sF2.enter_context(nc.sbuf_tensor("xnb3", [128, D], F32))
                    pg = [sF2.enter_context(nc.psum_tensor(f"pg{i}", [128, 512], F32))
                          for i in range(2)]
                    pu = [sF2.enter_context(nc.psum_tensor(f"pu{i}", [128, 512], F32))
                          for i in range(2)]

                    @block.vector
                    def _(v):
                        def quant(gq):
                            for n in range(4):
                                t = 4 * gq + n
                                if n == 0:
                                    s_xqw.wait_item(v, gq - 2)
                                    s_yq.wait_item(v, gq)
                                ys = yq[gq % 2][:, n * D:(n + 1) * D]
                                v.tensor_scalar(xnb2[:], ys, m_arr[:, t:t + 1],
                                                rs2_arr[:, t:t + 1],
                                                ALU.subtract, ALU.mult)
                                v.tensor_scalar(xnb3[:], xnb2[:], MAGIC, MAGIC,
                                                ALU.add, ALU.subtract)
                                c_xq.inc(v.tensor_scalar(
                                    xqst[gq % 2][:, n * D:(n + 1) * D], xnb3[:],
                                    127.0, -128.0, ALU.min, ALU.max), 1)

                        def hhwork(gq):
                            for ic in range(NIC):
                                gi = NIC * gq + ic
                                c_sg.wait(v, gi + 1)
                                c_gu.wait(v, gi + 1)
                                s_hw.wait_item(v, gi - 2)
                                c_hh.inc(v.scalar_tensor_tensor(
                                    hhst[gi % 2][:], pu[gi % 2][:],
                                    sus1[:, ic:ic + 1], sgb[gi % 2][:],
                                    ALU.mult, ALU.mult), 1)
                                v.reduce_max(mxb22[:, ic:ic + 1], hhst[gi % 2][:],
                                             AX.X, apply_absolute_value=True)
                            v.drain()
                            v.reduce_max(mxbb[:], mxb22[:], AX.X)
                            v.drain()
                            ins = v.tensor_max(mx2[:], mx2[:], mxbb[:])
                            if gq == NG - 1:
                                c_mx2.inc(ins, 1)

                        quant(0)
                        for gq in range(NG):
                            if gq + 1 < NG:
                                quant(gq + 1)
                            hhwork(gq)

                    @block.sync
                    def _(sy):
                        for gq in range(min(2, NG)):
                            s_yw.wait_upto(sy, 4 * (gq + 1) - 1)
                            s_yq.inc(sy.dma_start(
                                out=sb3(yq[gq % 2][:], 4),
                                in_=rows3(y_scr.ap()[gq * 512:(gq + 1) * 512, :])),
                                gq)
                        for gq in range(NG):
                            c_xq.wait(sy, 4 * (gq + 1))
                            if gq + 2 < NG:
                                s_yw.wait_upto(sy, 4 * (gq + 3) - 1)
                                s_yq.inc(sy.dma_start(
                                    out=sb3(yq[gq % 2][:], 4),
                                    in_=rows3(y_scr.ap()[(gq + 2) * 512:
                                                         (gq + 3) * 512, :])),
                                    gq + 2)
                            s_xqw.inc(sy.dma_start(
                                out=rows3(xq_scr.ap()[gq * 512:(gq + 1) * 512, :]),
                                in_=sb3(xqst[gq % 2][:], 4)), gq)
                            s_xqw.wait_item(sy, gq)
                            if gq >= 2:
                                c_gu.wait(sy, NIC * (gq - 1))
                            for hc in range(NHC):
                                s_xqr.inc(sy.dma_start(
                                    out=xqf[gq % 2][:, hc * 512:(hc + 1) * 512],
                                    in_=xq_scr.ap()[gq * 512:(gq + 1) * 512,
                                                    hc * 128:(hc + 1) * 128],
                                    transpose=True), gq)

                    @block.tensor
                    def _(te):
                        s_fwr.wait(te, 16 * 2 * NHC)
                        c_pev.wait(te, 8 * NTT)
                        c_k.wait(te, NEC * NTT)
                        c_e.wait(te, NEC * NTT)
                        c_kvf.wait(te, NEC * NTT)
                        for gq in range(NG):
                            s_xqr.wait_item(te, gq)
                            for ic in range(NIC):
                                gi = NIC * gq + ic
                                par = gi % 2
                                if gi >= 2:
                                    c_sg.wait(te, gi - 1)
                                    c_hh.wait(te, gi - 1)
                                for wT, psm, chn in ((wgT, pg, c_pg), (wuT, pu, c_gu)):
                                    for hc in range(NHC):
                                        last = te.matmul(
                                            psm[par][:],
                                            wT[:, hc * DI + ic * 128: hc * DI + (ic + 1) * 128],
                                            xqf[gq % 2][:, hc * 512:(hc + 1) * 512],
                                            start=(hc == 0), stop=(hc == NHC - 1))
                                    chn.inc(last, 1)

                    @block.scalar
                    def _(s):
                        c_s1.wait(s, 2)
                        for gq in range(NG):
                            for ic in range(NIC):
                                gi = NIC * gq + ic
                                c_pg.wait(s, gi + 1)
                                if gi >= 2:
                                    c_hh.wait(s, gi - 1)
                                c_sg.inc(s.activation(sgb[gi % 2][:], pg[gi % 2][:],
                                                      AF.Silu, bias=0.0,
                                                      scale=sgs1[:, ic:ic + 1]), 1)
                                if gi >= 1:
                                    pj = gi - 1
                                    c_hh.wait(s, pj + 1)
                                    s_hw.inc(s.dma_start(
                                        out=hh_scr.ap()[(pj % NIC) * 128:(pj % NIC + 1) * 128,
                                                        (pj // NIC) * 512:(pj // NIC + 1) * 512],
                                        in_=hhst[pj % 2][:]), pj)
                        pj = NIC * NG - 1
                        c_hh.wait(s, pj + 1)
                        s_hw.inc(s.dma_start(
                            out=hh_scr.ap()[(pj % NIC) * 128:(pj % NIC + 1) * 128,
                                            (pj // NIC) * 512:(pj // NIC + 1) * 512],
                            in_=hhst[pj % 2][:]), pj)

                    @block.gpsimd
                    def _(g):
                        c_mx2.wait(g, 1)
                        s_mx.inc(g.dma_start(out=mx2_i.ap(), in_=mx2[:]), 16)
                        g.wait_ge(s_mx.sem, 32)
                        s_cc.inc(g.collective_compute(
                            "AllReduce", ALU.max, replica_groups=ALL8,
                            ins=[mx2_i.ap().opt()], outs=[mx2_o.ap().opt()]), 1)

                    @block.vector
                    def _(v):
                        c_e1.wait(v, 2)
                        s_gx2.wait(v, 16)
                        v.reduce_max(gx2t[:], mxrow2[0:1, :], AX.X)
                        v.drain()
                        v.tensor_scalar(t11a[:], e1t2[:], 1.0, None, ALU.add)
                        v.drain()
                        v.reciprocal(t11b[:], t11a[:])
                        v.drain()
                        v.tensor_scalar(s2t[:], t11b[:], gx2t[0:1, 0:1], 1.0 / 127.0,
                                        ALU.mult, ALU.mult)
                        v.drain()
                        c_s2.inc(v.reciprocal(is2t[:], s2t[:]), 1)

                    @block.sync
                    def _(sy):
                        s_cc.wait(sy, NCH + 2)
                        s_gx2.inc(sy.dma_start(out=mxrow2[:], in_=mx2_o.ap()), 16)

                    @block.scalar
                    def _(s):
                        c_s2.wait(s, 1)
                        s_sc.inc(s.dma_start(out=sc_scr.ap()[0:1, 2:3], in_=s2t[:]), 16)
                        s_sc.inc(s.dma_start(out=sc_scr.ap()[0:1, 3:4], in_=is2t[:]), 16)

            sF3 = contextlib.ExitStack()
            with sF3:
                wdT = sF3.enter_context(nc.sbuf_tensor("wdT", [128, NIC * D], BF16))
                hhld = [sF3.enter_context(nc.sbuf_tensor(f"hhld{i}", [128, NIC * 512], BF16))
                        for i in range(2)]
                hbuf = [sF3.enter_context(nc.sbuf_tensor(f"hbuf{i}", [128, NIC * 512], BF16))
                        for i in range(2)]
                tqa = sF3.enter_context(nc.sbuf_tensor("tqa", [128, 512], F32))
                outst = [sF3.enter_context(nc.sbuf_tensor(f"outst{i}", [128, 512], F32))
                         for i in range(2)]
                yo = [sF3.enter_context(nc.sbuf_tensor(f"yo{i}", [128, 4 * D], BF16))
                      for i in range(2)]
                pd = [sF3.enter_context(nc.psum_tensor(f"pd{i}", [128, 512], F32))
                      for i in range(8)]

                @block.sync
                def _(sy):
                    c_gu.wait(sy, NIC * NG)
                    c_hh.wait(sy, NIC * NG)
                    s_hw.wait_upto(sy, NIC * NG - 1)
                    s_xqw.wait_upto(sy, NG - 1)
                    s_fww.wait_upto(sy, NW - 1)
                    for ic in range(NIC):
                        s_wdr.inc(sy.dma_start(
                            out=wdT[:, ic * D:(ic + 1) * D],
                            in_=wd_scr.ap()[0:D, ic * 128:(ic + 1) * 128],
                            transpose=True), 16)

                    def hld(gq):
                        s_hw.wait_upto(sy, NIC * (gq + 1) - 1)
                        if gq >= 2:
                            c_h.wait(sy, NIC * (gq - 1))
                            c_oev.wait(sy, 8 * (gq - 1))
                        for ic in range(NIC):
                            s_hr.inc(sy.dma_start(
                                out=hhld[gq % 2][:, ic * 512:(ic + 1) * 512],
                                in_=hh_scr.ap()[ic * 128:(ic + 1) * 128,
                                                gq * 512:(gq + 1) * 512]), gq)
                        s_yo.inc(sy.dma_start(
                            out=sb3(yo[gq % 2][:], 4),
                            in_=rows3(y_scr.ap()[gq * 512:(gq + 1) * 512, :])), gq)

                    for gq in range(min(2, NG)):
                        hld(gq)
                    s_sc.wait(sy, 64)
                    s_gx2.inc(sy.dma_start(out=bc_s2[:], in_=bcast_ap(sc_scr, 2)), 16)
                    s_gx2.inc(sy.dma_start(out=bc_is2[:], in_=bcast_ap(sc_scr, 3)), 16)
                    for gq in range(2, NG):
                        hld(gq)
                    s_out.wait_upto(sy, 2 * NOB - 1)

                @block.vector
                def _(v):
                    s_gx2.wait(v, 48)

                    def quant(gq):
                        for ic in range(NIC):
                            gi = NIC * gq + ic
                            if ic == 0:
                                s_hr.wait_item(v, gq)
                                if gq >= 2:
                                    c_dn.wait(v, 8 * (gq - 1))
                            v.tensor_scalar(tqa[:], hhld[gq % 2][:, ic * 512:(ic + 1) * 512],
                                            bc_is2[:, 0:1], MAGIC, ALU.mult, ALU.add)
                            v.tensor_scalar(tqa[:], tqa[:], MAGIC, 127.0,
                                            ALU.subtract, ALU.min)
                            c_h.inc(v.tensor_scalar(
                                hbuf[gq % 2][:, ic * 512:(ic + 1) * 512], tqa[:],
                                -128.0, None, ALU.max), 1)

                    def outevac(gq, h):
                        if h == 0:
                            s_yo.wait_item(v, gq)
                        for n in (2 * h, 2 * h + 1):
                            t = 4 * gq + n
                            for hf in range(2):
                                idx = 2 * t + hf
                                s_out.wait_item(v, idx - 2)
                                c_dn.wait(v, idx + 1)
                                c_oev.inc(v.scalar_tensor_tensor(
                                    outst[idx % 2][:],
                                    pd[idx % 8][:], bc_s2[:, 0:1],
                                    yo[gq % 2][:, n * D + hf * 512:
                                               n * D + (hf + 1) * 512],
                                    ALU.mult, ALU.add), 1)

                    quant(0)
                    for gq in range(NG):
                        if gq + 1 < NG:
                            outevac(gq, 0)
                            quant(gq + 1)
                            outevac(gq, 1)
                        else:
                            outevac(gq, 0)
                            outevac(gq, 1)

                @block.tensor
                def _(te):
                    s_wdr.wait(te, 16 * NIC)
                    c_sg.wait(te, NIC * NG)
                    c_hh.wait(te, NIC * NG)
                    for gq in range(NG):
                        for h in range(2):
                            if gq >= 1:
                                c_oev.wait(te, 8 * (gq - 1) + 4 * (h + 1))
                            for ic in range(NIC):
                                if h == 0:
                                    c_h.wait(te, NIC * gq + ic + 1)
                                for n in (2 * h, 2 * h + 1):
                                    t = 4 * gq + n
                                    for hf in range(2):
                                        idx = 2 * t + hf
                                        last = te.matmul(
                                            pd[idx % 8][:],
                                            hbuf[gq % 2][:, ic * 512 + n * 128:
                                                          ic * 512 + (n + 1) * 128],
                                            wdT[:, ic * D + hf * 512: ic * D + (hf + 1) * 512],
                                            start=(ic == 0), stop=(ic == NIC - 1))
                                        if ic == NIC - 1:
                                            c_dn.inc(last, 1)

                @block.scalar
                def _(s):
                    for t in range(NOB):
                        for hf in range(2):
                            idx = 2 * t + hf
                            c_oev.wait(s, idx + 1)
                            s_out.inc(s.dma_start(
                                out=out_d.ap()[t * 128:(t + 1) * 128,
                                               hf * 512:(hf + 1) * 512],
                                in_=outst[idx % 2][:]), idx)

    return nc


_NC_CACHE = {}


def _get_nc(LQ, D, DI):
    key = (LQ, D, DI)
    if key not in _NC_CACHE:
        _NC_CACHE[key] = build_nc(LQ, D, DI)
    return _NC_CACHE[key]


def kernel(L0_attn_X, L0_attn_Wq, L0_attn_Wk, L0_attn_Wv, L0_attn_Wc,
           L0_ffn_clip_x, L0_ffn_clip_down, L0_ffn_w_gate, L0_ffn_w_up,
           L0_ffn_w_down):
    X = np.ascontiguousarray(np.asarray(L0_attn_X, dtype=np.float32))
    Wq = np.ascontiguousarray(np.asarray(L0_attn_Wq, dtype=np.float32))
    Wk = np.ascontiguousarray(np.asarray(L0_attn_Wk, dtype=np.float32))
    Wv = np.ascontiguousarray(np.asarray(L0_attn_Wv, dtype=np.float32))
    Wc = np.ascontiguousarray(np.asarray(L0_attn_Wc, dtype=np.float32))
    Wg = np.ascontiguousarray(np.asarray(L0_ffn_w_gate, dtype=np.float32))
    Wu = np.ascontiguousarray(np.asarray(L0_ffn_w_up, dtype=np.float32))
    Wd = np.ascontiguousarray(np.asarray(L0_ffn_w_down, dtype=np.float32))
    cx = np.asarray(L0_ffn_clip_x, dtype=np.float32).reshape(1, 1)
    cd = np.asarray(L0_ffn_clip_down, dtype=np.float32).reshape(1, 1)

    B, LQ, D = X.shape
    DI = Wg.shape[0]
    EHW = D // 2
    q4 = LQ // 4
    SPANS = [(0, q4), (q4, 2 * q4), (2 * q4, 3 * q4),
             (3 * q4, 3 * q4 + q4 // 2), (3 * q4 + q4 // 2, 4 * q4)]

    def own_rows(eh):
        return np.concatenate(
            [np.arange(s + eh * (e - s) // 2, s + (eh + 1) * (e - s) // 2)
             for s, e in SPANS])

    nc = _get_nc(LQ, D, DI)
    in_maps = []
    for c in range(8):
        b, eh = c >> 1, c & 1
        in_maps.append(dict(
            xb=X[b],
            x_own=np.ascontiguousarray(X[b][own_rows(eh)]),
            wq=np.ascontiguousarray(Wq[:, eh * EHW:(eh + 1) * EHW]),
            wk=np.ascontiguousarray(Wk[:, eh * EHW:(eh + 1) * EHW]),
            wv=np.ascontiguousarray(Wv[:, eh * EHW:(eh + 1) * EHW]),
            wc=np.ascontiguousarray(Wc[eh * EHW:(eh + 1) * EHW, :]),
            wg=Wg, wu=Wu, wd=Wd, clipx=cx, clipd=cd,
        ))
    res = run_bass_kernel_spmd(nc, in_maps, core_ids=list(range(8)))
    out = np.empty((B, LQ, D), dtype=np.float32)
    for c in range(8):
        b, eh = c >> 1, c & 1
        out[b, own_rows(eh)] = np.asarray(res.results[c]["out"])
    return out


# version 22
# speedup vs baseline: 1.0246x; 1.0065x over previous
import contextlib
import numpy as np

import concourse.bass as bass
import concourse.mybir as mybir
from concourse.bass_utils import run_bass_kernel_spmd

F32 = mybir.dt.float32
BF16 = mybir.dt.bfloat16
AF = mybir.ActivationFunctionType
ALU = mybir.AluOpType
AX = mybir.AxisListType

MAGIC = float(np.float32(3 * 2**22))
EPS_LN = 1e-5
EPS_ATTN = 1e-6
PAIRS = [[0, 1], [2, 3], [4, 5], [6, 7]]
ALL8 = [list(range(8))]


class Ch:
    def __init__(self, es, nc, name):
        self.sem = es.enter_context(nc.semaphore(name))
        self.n = 0

    def inc(self, instr, k=1):
        instr.then_inc(self.sem, k)
        self.n += k
        return instr

    def wait(self, eng, v):
        if v > 0:
            eng.wait_ge(self.sem, v)


class ChP:

    def __init__(self, es, nc, name, per=16, nring=2):
        self.s = [es.enter_context(nc.semaphore(f"{name}{i}"))
                  for i in range(nring)]
        self.per = per
        self.nr = nring

    def inc(self, instr, item, k=16):
        instr.then_inc(self.s[item % self.nr], k)
        return instr

    def wait_item(self, eng, item):
        if item < 0:
            return
        eng.wait_ge(self.s[item % self.nr], self.per * (item // self.nr + 1))

    def wait_upto(self, eng, item):
        for i in range(self.nr):
            self.wait_item(eng, item - i)


def bcast_ap(handle, off):
    return bass.AP(handle, off, [[0, 128], [1, 1]])


def rows3(ap2d):
    return ap2d.rearrange("(n p) d -> p n d", p=128)


def sb3(ap2d, n):
    return ap2d.rearrange("p (n d) -> p n d", n=n)


def build_nc(LQ=8192, D=1024, DI=2816, debug=False):
    TOK = LQ // 2
    NTT = LQ // 512
    NA = 4 * NTT
    ND = D // 128
    EHW = D // 2
    NEC = EHW // 128
    NIC = DI // 128
    NHC = D // 128
    NG = TOK // 512
    NOB = TOK // 128
    NW = 2 * NIC + NHC
    NQF = NOB // 4
    q4 = LQ // 4
    SPANS = [(0, q4), (q4, 2 * q4), (2 * q4, 3 * q4),
             (3 * q4, 3 * q4 + q4 // 2), (3 * q4 + q4 // 2, 4 * q4)]
    NCH = len(SPANS)
    CIDX = []
    for _t in range(NOB):
        for _ci, (_s, _e) in enumerate(SPANS):
            if _t * 128 < _e // 2:
                CIDX.append(_ci)
                break

    def ps2_row(t0):
        return t0

    nc = bass.Bass("TRN2", target_bir_lowering=False,
                   detect_race_conditions=False)

    xb = nc.dram_tensor("xb", [LQ, D], F32, kind="ExternalInput")
    x_own = nc.dram_tensor("x_own", [TOK, D], F32, kind="ExternalInput")
    wq_d = nc.dram_tensor("wq", [D, EHW], F32, kind="ExternalInput")
    wk_d = nc.dram_tensor("wk", [D, EHW], F32, kind="ExternalInput")
    wv_d = nc.dram_tensor("wv", [D, EHW], F32, kind="ExternalInput")
    wc_d = nc.dram_tensor("wc", [EHW, D], F32, kind="ExternalInput")
    wg_d = nc.dram_tensor("wg", [DI, D], F32, kind="ExternalInput")
    wu_d = nc.dram_tensor("wu", [DI, D], F32, kind="ExternalInput")
    wd_d = nc.dram_tensor("wd", [D, DI], F32, kind="ExternalInput")
    clipx_d = nc.dram_tensor("clipx", [1, 1], F32, kind="ExternalInput")
    clipd_d = nc.dram_tensor("clipd", [1, 1], F32, kind="ExternalInput")
    out_d = nc.dram_tensor("out", [TOK, D], F32, kind="ExternalOutput")

    dk = "ExternalOutput" if debug else "Internal"
    xn_scr = nc.dram_tensor("xn_scr", [LQ, D], BF16, kind=dk)
    ps_scr = nc.dram_tensor("ps_scr", [LQ, D], BF16)
    rs_scr = nc.dram_tensor("rs_scr", [TOK, D], BF16)
    xq_scr = nc.dram_tensor("xq_scr", [TOK, D], BF16, kind=dk)
    wg_scr = nc.dram_tensor("wg_scr", [DI, D], BF16, kind=dk)
    wu_scr = nc.dram_tensor("wu_scr", [DI, D], BF16, kind=dk)
    wd_scr = nc.dram_tensor("wd_scr", [D, DI], BF16, kind=dk)
    hh_scr = nc.dram_tensor("hh_scr", [DI, TOK], BF16, kind=dk)
    y_scr = nc.dram_tensor("y_scr", [TOK, D], BF16, kind=dk)
    mx1_i = nc.dram_tensor("mx1_i", [1, 128], F32)
    mx1_o = nc.dram_tensor("mx1_o", [1, 128], F32)
    mx2_i = nc.dram_tensor("mx2_i", [1, 128], F32)
    mx2_o = nc.dram_tensor("mx2_o", [1, 128], F32)
    sc_scr = nc.dram_tensor("sc_scr", [1, 8], F32, kind=dk)

    es = contextlib.ExitStack()
    with es:
        def sb(name, shape, dt=F32):
            return es.enter_context(nc.sbuf_tensor(name, shape, dt))

        def ch(name):
            return Ch(es, nc, name)

        biase = sb("biase", [128, 1], F32)
        biasD2e = sb("biasD2e", [128, 1], F32)
        biaslnD = sb("biaslnD", [128, 1], F32)
        biasM = sb("biasM", [128, 1], F32)
        bp_s = sb("bp_s", [128, 12], F32)
        ssq_s = sb("ssq_s", [128, 12], F32)
        mraw_s = sb("mraw_s", [128, 12], F32)
        m_s = sb("m_s", [128, 12], F32)
        m2_s = sb("m2_s", [128, 12], F32)
        var_s = sb("var_s", [128, 12], F32)
        rstd_s = sb("rstd_s", [128, 12], F32)
        lnv_s = sb("lnv_s", [128, 12], F32)
        mx1 = sb("mx1", [128, 1], F32)
        mx2 = sb("mx2", [128, 1], F32)
        mxb4 = sb("mxb4", [128, 4], F32)
        mxbb = sb("mxbb", [128, 1], F32)
        mxb22 = sb("mxb22", [128, NIC], F32)
        mxrow = sb("mxrow", [1, 128], F32)
        mxrow2 = sb("mxrow2", [1, 128], F32)
        scl_x = sb("scl_x", [1, 1], F32)
        scl_d = sb("scl_d", [1, 1], F32)
        e1t = sb("e1t", [1, 1], F32)
        e1t2 = sb("e1t2", [1, 1], F32)
        gxt = sb("gxt", [1, 1], F32)
        gx2t = sb("gx2t", [1, 1], F32)
        t11a = sb("t11a", [1, 1], F32)
        t11b = sb("t11b", [1, 1], F32)
        s1t = sb("s1t", [1, 1], F32)
        is1t = sb("is1t", [1, 1], F32)
        s2t = sb("s2t", [1, 1], F32)
        is2t = sb("is2t", [1, 1], F32)
        bc_s1 = sb("bc_s1", [128, 1], F32)
        bc_is1 = sb("bc_is1", [128, 1], F32)
        bc_s2 = sb("bc_s2", [128, 1], F32)
        ssqF = sb("ssqF", [128, 8], F32)
        mrawF = sb("mrawF", [128, 8], F32)
        m2F = sb("m2F", [128, 8], F32)
        varF = sb("varF", [128, 8], F32)
        lnvF = sb("lnvF", [128, 8], F32)
        mxq4 = sb("mxq4", [128, 4], F32)
        mxb8 = sb("mxb8", [128, 8], F32)
        junk = sb("junk", [128, D], BF16)
        xo_h = [sb(f"xo_h{i}", [128, 512], F32) for i in range(2)]
        rs_h = [sb(f"rs_h{i}", [128, 512], BF16) for i in range(2)]
        yst = [sb(f"yst{i}", [128, D], BF16) for i in range(8)]
        xnb1 = sb("xnb1", [128, D], BF16)
        m_arr = sb("m_arr", [128, NOB], F32)
        rstd2_arr = sb("rstd2_arr", [128, NOB], F32)
        rs2_arr = sb("rs2_arr", [128, NOB], F32)
        rmx = sb("rmx", [128, 4], F32)
        rcp = sb("rcp", [128, 4], F32)
        sdd = sb("sdd", [128, 4], F32)
        swg_arr = sb("swg_arr", [128, NIC], F32)
        swu_arr = sb("swu_arr", [128, NIC], F32)
        sgs1 = sb("sgs1", [128, NIC], F32)
        sus1 = sb("sus1", [128, NIC], F32)
        bc_is2 = sb("bc_is2", [128, 1], F32)

        def chp(name, per=16, nring=2):
            return ChP(es, nc, name, per=per, nring=nring)

        s_x = chp("s_x", nring=3); s_xnw = chp("s_xnw"); s_xnr = chp("s_xnr", per=16 * ND)
        s_pw = chp("s_pw")
        s_fw = chp("s_fw", nring=4); s_fww = chp("s_fww", nring=4)
        s_rs = chp("s_rs"); s_xo = chp("s_xo")
        s_xqw = chp("s_xqw"); s_xqr = chp("s_xqr", per=16 * NHC)
        s_hw = chp("s_hw"); s_out = chp("s_out")
        s_hr = chp("s_hr", per=16 * NIC)
        s_clip = ch("s_clip"); s_aw = chp("s_aw", nring=4)
        s_fwr = ch("s_fwr"); s_wdr = ch("s_wdr")
        s_cc = ch("s_cc"); s_mx = ch("s_mx")
        s_gx1 = ch("s_gx1"); s_gx2 = ch("s_gx2")
        s_sc = ch("s_sc")

        c_init = ch("c_init"); c_sq = ch("c_sq"); c_var = ch("c_var")
        c_rstd = ch("c_rstd"); c_xn = ch("c_xn"); c_wcast = ch("c_wcast")
        c_pk = ch("c_pk"); c_pv = ch("c_pv"); c_pq = ch("c_pq")
        c_k = ch("c_k"); c_e = ch("c_e"); c_kvf = ch("c_kvf")
        c_kfree = ch("c_kfree"); c_epk = ch("c_epk"); c_d = ch("c_d")
        c_O = ch("c_O"); c_wc = ch("c_wc"); c_pev = ch("c_pev")
        c_teT = ch("c_teT"); c_wq = ch("c_wq"); c_y = ch("c_y")
        c_f1 = ch("c_f1"); c_e1 = ch("c_e1"); c_s1 = ch("c_s1")
        c_xq = ch("c_xq"); c_pg = ch("c_pg"); c_gu = ch("c_gu")
        c_sg = ch("c_sg"); c_hh = ch("c_hh"); c_mx2 = ch("c_mx2")
        c_s2 = ch("c_s2"); c_h = ch("c_h"); c_dn = ch("c_dn")
        c_oev = ch("c_oev"); c_bp = ch("c_bp"); c_rcp = ch("c_rcp")
        c_wt1 = ch("c_wt1")
        c_yF = ch("c_yF"); c_sqF = ch("c_sqF"); c_varF = ch("c_varF")
        c_rstdF = ch("c_rstdF"); c_mxF = ch("c_mxF")
        s_yw = chp("s_yw"); s_yq = chp("s_yq"); s_yo = chp("s_yo")

        def f1_g(g, t):
            for hf in range(2):
                i2 = 2 * t + hf
                s_xo.inc(g.dma_start(
                    out=xo_h[i2 % 2][:],
                    in_=x_own.ap()[t * 128:(t + 1) * 128,
                                   hf * 512:(hf + 1) * 512]), i2)
            s_cc.wait(g, CIDX[t] + 1)
            for hf in range(2):
                i2 = 2 * t + hf
                s_rs.inc(g.dma_start(
                    out=rs_h[i2 % 2][:],
                    in_=rs_scr.ap()[t * 128:(t + 1) * 128,
                                    hf * 512:(hf + 1) * 512]), i2)
            s_yw.wait_item(g, t - 8)
            if t >= 8:
                c_sqF.wait(g, t - 7)
                c_mxF.wait(g, t - 7)
            ins = None
            for hf in range(2):
                i2 = 2 * t + hf
                s_xo.wait_item(g, i2)
                s_rs.wait_item(g, i2)
                ins = g.tensor_add(yst[t % 8][:, hf * 512:(hf + 1) * 512],
                                   xo_h[i2 % 2][:], rs_h[i2 % 2][:])
            c_yF.inc(ins, 1)
            s_yw.inc(g.dma_start(out=y_scr.ap()[t * 128:(t + 1) * 128, :],
                                 in_=yst[t % 8][:]), t)

        def f1_sa(s, u):
            grF = (u % 2) * 4
            c_varF.wait(s, u - 1)
            for n in range(4):
                t = 4 * u + n
                c_yF.wait(s, t + 1)
                s.activation(junk[:], yst[t % 8][:], AF.Square, bias=0.0,
                             scale=1.0, accum_out=ssqF[:, grF + n:grF + n + 1])
                c_sqF.inc(s.activation(
                    junk[:], yst[t % 8][:], AF.Identity, bias=0.0, scale=1.0,
                    accum_out=mrawF[:, grF + n:grF + n + 1]), 1)

        def f1_sb(s, u):
            grF = (u % 2) * 4
            c_varF.wait(s, u + 1)
            s.activation(lnvF[:, grF:grF + 4], varF[:, grF:grF + 4], AF.Ln,
                         bias=biase[:], scale=1.0)
            s.drain()
            c_rstdF.inc(s.activation(
                rstd2_arr[:, 4 * u:4 * u + 4], lnvF[:, grF:grF + 4],
                AF.Exp, bias=0.0, scale=-0.5), 1)

        def f1_da(v, u):
            grF = (u % 2) * 4
            c_sqF.wait(v, 4 * (u + 1))
            v.tensor_scalar(m_arr[:, 4 * u:4 * u + 4], mrawF[:, grF:grF + 4],
                            1.0 / D, None, ALU.mult)
            c_rstdF.wait(v, u - 1)
            v.drain()
            v.tensor_mul(m2F[:, grF:grF + 4], m_arr[:, 4 * u:4 * u + 4],
                         m_arr[:, 4 * u:4 * u + 4])
            v.drain()
            c_varF.inc(v.scalar_tensor_tensor(
                varF[:, grF:grF + 4], ssqF[:, grF:grF + 4], 1.0 / D,
                m2F[:, grF:grF + 4], ALU.mult, ALU.subtract), 1)
            v.drain()
            for n in range(4):
                t = 4 * u + n
                v.tensor_scalar(xnb1[:], yst[t % 8][:], m_arr[:, t:t + 1],
                                None, ALU.subtract)
                c_mxF.inc(v.reduce_max(mxb8[:, grF + n:grF + n + 1], xnb1[:],
                                       AX.X, apply_absolute_value=True), 1)

        def f1_db(v, u):
            grF = (u % 2) * 4
            c_rstdF.wait(v, u + 1)
            v.drain()
            v.tensor_mul(mxq4[:], mxb8[:, grF:grF + 4],
                         rstd2_arr[:, 4 * u:4 * u + 4])
            v.drain()
            v.reduce_max(mxbb[:], mxq4[:], AX.X)
            v.drain()
            ins = v.tensor_max(mx1[:], mx1[:], mxbb[:])
            if u == NQF - 1:
                c_f1.inc(ins, 1)

        if NTT == 16:
            ACT_MAP = {12: [("sa", 0)], 13: [("sa", 1), ("sb", 0)],
                       14: [("sa", 2), ("sb", 1)], 15: [("sa", 3), ("sb", 2)]}
            DVE_MAP = {13: [("da", 0)], 14: [("da", 1), ("db", 0)],
                       15: [("da", 2), ("db", 1)]}
            ACT_POST = [("sa", 4), ("sb", 3), ("sa", 5), ("sb", 4),
                        ("sa", 6), ("sb", 5), ("sa", 7), ("sb", 6), ("sb", 7)]
            DVE_POST = [("da", 3), ("db", 2), ("da", 4), ("db", 3),
                        ("da", 5), ("db", 4), ("da", 6), ("db", 5),
                        ("da", 7), ("db", 6), ("db", 7)]
        else:
            ACT_MAP = {}
            DVE_MAP = {}
            ACT_POST = []
            DVE_POST = []
            for u in range(NQF):
                ACT_POST += [("sa", u), ("sb", u)]
                DVE_POST += [("da", u), ("db", u)]

        def f1_act(s, items):
            for kind, u in items:
                (f1_sa if kind == "sa" else f1_sb)(s, u)

        def f1_dve(v, items):
            for kind, u in items:
                (f1_da if kind == "da" else f1_db)(v, u)

        block = es.enter_context(nc.Block())

        sA = contextlib.ExitStack()
        with sA:
            def sba(name, shape, dt=F32):
                return sA.enter_context(nc.sbuf_tensor(name, shape, dt))

            ones_bf = sba("ones_bf", [128, 512], BF16)
            x_t = [sba(f"x_t{i}", [128, 4 * D]) for i in range(3)]
            xnst = [sba(f"xnst{i}", [128, 4 * D], BF16) for i in range(2)]
            xnf = [sba(f"xnf{i}", [128, ND * 512], BF16) for i in range(2)]
            wk_sb = sba("wk_sb", [128, ND * EHW], BF16)
            wv_sb = sba("wv_sb", [128, ND * EHW], BF16)
            wq_sb = sba("wq_sb", [128, ND * EHW], BF16)
            wc_sb = sba("wc_sb", [128, NEC * D], BF16)
            K2 = [sba(f"K2_{i}", [128, 512], BF16) for i in range(2)]
            e2 = [sba(f"e2_{i}", [128, 512], BF16) for i in range(2)]
            KVm = sba("KVm", [128, 512], BF16)
            cumK = sba("cumK", [128, NEC * 512])
            cumKV = sba("cumKV", [128, NEC * 512])
            epk = [sba(f"epk{i}", [128, 512], BF16) for i in range(2)]
            dln = sba("dln", [128, 512])
            dex = [sba(f"dex{i}", [128, 512], BF16) for i in range(2)]
            Obuf = [sba(f"Obuf{i}", [128, NEC * 512], BF16) for i in range(2)]
            pst = [sba(f"pst{i}", [128, D], BF16) for i in range(2)]
            wld = [sba(f"wld{i}", [128, D]) for i in range(4)]
            wqst = [sba(f"wqst{i}", [128, D], BF16) for i in range(4)]

            pk = [sA.enter_context(nc.psum_tensor(f"pk{i}", [128, 512], F32)) for i in range(2)]
            pv = [sA.enter_context(nc.psum_tensor(f"pv{i}", [128, 512], F32)) for i in range(2)]
            pq = [sA.enter_context(nc.psum_tensor(f"pq{i}", [128, 512], F32)) for i in range(2)]
            pwc = [sA.enter_context(nc.psum_tensor(f"pwc{i}", [128, 512], F32)) for i in range(2)]

            AWS = ((wq_d, wq_sb, ND, EHW),
                   (wk_d, wk_sb, ND, EHW),
                   (wv_d, wv_sb, ND, EHW),
                   (wc_d, wc_sb, NEC, D))

            PREP = ([(wg_d, wg_scr, D, "g", ic) for ic in range(NIC)]
                    + [(wu_d, wu_scr, D, "u", ic) for ic in range(NIC)])
            NGU = len(PREP)
            NPQ = (NGU + 3) // 4

            @block.gpsimd
            def _(g):
                s_clip.inc(g.dma_start(out=scl_x[:], in_=clipx_d.ap()), 16)
                s_clip.inc(g.dma_start(out=scl_d[:], in_=clipd_d.ap()), 16)
                for wi, (wdram, _sb, nch, width) in enumerate(AWS):
                    nq_ = nch // 4
                    for qt in range(4):
                        c_wcast.wait(g, 4 * wi + qt - 3)
                        rs_ = wdram.ap()[qt * nq_ * 128:(qt + 1) * nq_ * 128, :]
                        s_aw.inc(g.dma_start(
                            out=sb3(wld[qt][:, 0:nq_ * width], nq_),
                            in_=rows3(rs_)), 4 * wi + qt)

                def rs_chunk(c):
                    sps, spe = SPANS[c]
                    s_pw.wait_upto(g, spe // 128 - 1)
                    s_cc.inc(g.collective_compute(
                        "ReduceScatter", ALU.add, replica_groups=PAIRS,
                        ins=[ps_scr.ap()[sps:spe, :].opt()],
                        outs=[rs_scr.ap()[sps // 2:spe // 2, :].opt()]), 1)

                def prep_load(j):
                    wdram, _scr, cols, kind, row = PREP[j]
                    if j >= 4:
                        c_wq.wait(g, j - 3)
                    else:
                        c_wcast.wait(g, 16)
                    s_fw.inc(g.dma_start(
                        out=wld[j % 4][:, 0:cols],
                        in_=wdram.ap()[row * 128:(row + 1) * 128, :]), j)

                if NTT == 16 and NGU == 44 and NOB == 32:
                    SEQ = ([("j", j) for j in range(16)] + [("c", 0)]
                           + [("j", j) for j in range(16, 44)]
                           + [("f", t) for t in range(0, 8)] + [("c", 1)]
                           + [("f", t) for t in range(8, 12)] + [("c", 2)]
                           + [("f", t) for t in range(12, 20)] + [("c", 3)]
                           + [("f", t) for t in range(20, 24)] + [("c", 4)]
                           + [("f", t) for t in range(24, 32)])
                else:
                    SEQ = ([("j", j) for j in range(NGU)]
                           + [("c", c) for c in range(NCH)]
                           + [("f", t) for t in range(NOB)])
                for kind_, i_ in SEQ:
                    if kind_ == "j":
                        prep_load(i_)
                    elif kind_ == "c":
                        rs_chunk(i_)
                    else:
                        f1_g(g, i_)

            @block.vector
            def _(v):
                v.memset(ones_bf[:], 1.0)
                v.memset(biase[:], EPS_LN)
                v.memset(biasD2e[:], float(D) * float(D) * EPS_LN)
                v.memset(biaslnD[:], float(np.log(D)))
                v.memset(biasM[:], MAGIC)
                v.memset(mx1[:], 0.0)
                v.memset(mx2[:], 0.0)
                v.memset(mxrow[:], 0.0)
                c_init.inc(v.memset(mxrow2[:], 0.0), 1)
                for wi, (_w, wsb, nch, width) in enumerate(AWS):
                    qw_ = (nch // 4) * width
                    for qt in range(4):
                        s_aw.wait_item(v, 4 * wi + qt)
                        c_wcast.inc(v.tensor_copy(
                            wsb[:, qt * qw_:(qt + 1) * qw_],
                            wld[qt][:, 0:qw_]), 1)

                def prep_quad(qd):
                    tiles = list(range(4 * qd, min(4 * qd + 4, NGU)))
                    for j in tiles:
                        s_fw.wait_item(v, j)
                        s_fww.wait_item(v, j - 4)
                        v.reduce_max(rmx[:, j % 4:j % 4 + 1],
                                     wld[j % 4][:, 0:PREP[j][2]], AX.X,
                                     apply_absolute_value=True)
                    v.drain()
                    for j in tiles:
                        _w, _scr, cols, kind, row = PREP[j]
                        sd_ = swg_arr[:, row:row + 1] if kind == "g" else swu_arr[:, row:row + 1]
                        v.tensor_scalar(sd_, rmx[:, j % 4:j % 4 + 1],
                                        1.0 / 127.0, None, ALU.mult)
                    v.drain()
                    for j in tiles:
                        _w, _scr, cols, kind, row = PREP[j]
                        sd_ = swg_arr[:, row:row + 1] if kind == "g" else swu_arr[:, row:row + 1]
                        ins_ = v.reciprocal(rcp[:, j % 4:j % 4 + 1], sd_)
                    c_rcp.inc(ins_, 1)
                    for j in tiles:
                        cols_ = PREP[j][2]
                        c_wt1.wait(v, j + 1)
                        c_wq.inc(v.tensor_scalar(
                            wqst[j % 4][:, 0:cols_], wld[j % 4][:, 0:cols_],
                            MAGIC, None, ALU.subtract), 1)

                def stats_xn(tt):
                    gr = (tt % 3) * 4
                    s_x.wait_item(v, tt)
                    s_xnw.wait_item(v, tt - 2)
                    c_sq.wait(v, 4 * (tt + 1))
                    v.tensor_mul(m2_s[:, gr:gr + 4], mraw_s[:, gr:gr + 4],
                                 mraw_s[:, gr:gr + 4])
                    c_rstd.wait(v, tt - 2)
                    v.drain()
                    c_var.inc(v.scalar_tensor_tensor(
                        var_s[:, gr:gr + 4], ssq_s[:, gr:gr + 4], float(D),
                        m2_s[:, gr:gr + 4], ALU.mult, ALU.subtract), 1)
                    c_rstd.wait(v, tt + 1)
                    c_bp.inc(v.scalar_tensor_tensor(
                        bp_s[:, gr:gr + 4], mraw_s[:, gr:gr + 4], -1.0 / D,
                        rstd_s[:, gr:gr + 4], ALU.mult, ALU.mult), 1)
                    v.drain()
                    for n in range(4):
                        c_xn.inc(v.tensor_scalar(
                            xnst[tt % 2][:, n * D:(n + 1) * D],
                            x_t[tt % 3][:, n * D:(n + 1) * D],
                            rstd_s[:, gr + n:gr + n + 1],
                            bp_s[:, gr + n:gr + n + 1],
                            ALU.mult, ALU.add), 1)

                def emit_O(g):
                    tt_, ec_ = g // NEC, g % NEC
                    c_d.wait(v, g + 1)
                    if ec_ == 0 and tt_ >= 2:
                        c_wc.wait(v, 8 * (tt_ - 1))
                    c_O.inc(v.tensor_mul(
                        Obuf[tt_ % 2][:, ec_ * 512:(ec_ + 1) * 512],
                        cumKV[:, ec_ * 512:(ec_ + 1) * 512], dex[g % 2][:]), 1)

                def chain(tt):
                    for ec in range(NEC):
                        g = NEC * tt + ec
                        par = g % 2
                        ck = cumK[:, ec * 512:(ec + 1) * 512]
                        ckv = cumKV[:, ec * 512:(ec + 1) * 512]
                        c_k.wait(v, g + 1)
                        c_pv.wait(v, g + 1)
                        c_kvf.inc(v.tensor_mul(KVm[:], K2[par][:], pv[par][:]), 1)
                        initK = EPS_ATTN if tt == 0 else ck[:, 511:512]
                        c_kfree.inc(v.tensor_tensor_scan(
                            ck, ones_bf[:], K2[par][:], initK, ALU.mult, ALU.add), 1)
                        initV = 0.0 if tt == 0 else ckv[:, 511:512]
                        v.tensor_tensor_scan(ckv, ones_bf[:], KVm[:], initV,
                                             ALU.mult, ALU.add)
                        c_e.wait(v, g + 1)
                        c_epk.inc(v.scalar_tensor_tensor(
                            epk[par][:], e2[par][:], 1.0, ck,
                            ALU.add, ALU.mult), 1)
                        if g >= 1:
                            emit_O(g - 1)

                stats_xn(0)
                if NTT > 1:
                    stats_xn(1)
                for tt in range(NTT):
                    if tt + 2 < NTT:
                        stats_xn(tt + 2)
                    chain(tt)
                    f1_dve(v, DVE_MAP.get(tt, []))
                    if tt < NPQ:
                        prep_quad(tt)
                emit_O(NEC * NTT - 1)
                for qd in range(NTT, NPQ):
                    prep_quad(qd)

            @block.scalar
            def _(s):
                s_clip.wait(s, 32)
                c_e1.inc(s.activation(e1t[:], scl_x[:], AF.Exp, bias=0.0, scale=-1.0), 1)
                c_e1.inc(s.activation(e1t2[:], scl_d[:], AF.Exp, bias=0.0, scale=-1.0), 1)
                c_init.wait(s, 1)

                def stats_act(tt):
                    gr = (tt % 3) * 4
                    s_x.wait_item(s, tt)
                    c_var.wait(s, tt - 2)
                    c_bp.wait(s, tt - 2)
                    for n in range(4):
                        s.activation(junk[:], x_t[tt % 3][:, n * D:(n + 1) * D],
                                     AF.Square, bias=0.0, scale=1.0,
                                     accum_out=ssq_s[:, gr + n:gr + n + 1])
                        c_sq.inc(s.activation(junk[:], x_t[tt % 3][:, n * D:(n + 1) * D],
                                              AF.Identity, bias=0.0, scale=1.0,
                                              accum_out=mraw_s[:, gr + n:gr + n + 1]), 1)
                    c_var.wait(s, tt + 1)
                    s.activation(lnv_s[:, gr:gr + 4], var_s[:, gr:gr + 4],
                                 AF.Ln, bias=biasD2e[:], scale=1.0)
                    s.drain()
                    c_rstd.inc(s.activation(rstd_s[:, gr:gr + 4], lnv_s[:, gr:gr + 4],
                                            AF.Exp, bias=biaslnD[:], scale=-0.5), 1)


                def emit_lnd(g):
                    c_epk.wait(s, g + 1)
                    s.activation(dln[:], epk[g % 2][:], AF.Ln, bias=0.0, scale=1.0)
                    if g >= 2:
                        c_O.wait(s, g - 1)
                    c_d.inc(s.activation(dex[g % 2][:], dln[:], AF.Exp,
                                         bias=0.0, scale=-1.0), 1)

                def chain_act(tt):
                    for ec in range(NEC):
                        g = NEC * tt + ec
                        par = g % 2
                        c_pk.wait(s, g + 1)
                        if g >= 2:
                            c_kfree.wait(s, g - 1)
                        c_k.inc(s.activation(K2[par][:], pk[par][:], AF.Exp,
                                             bias=0.0, scale=1.0), 1)
                        c_pq.wait(s, g + 1)
                        if g >= 2:
                            c_epk.wait(s, g - 1)
                        c_e.inc(s.activation(e2[par][:], pq[par][:], AF.Exp,
                                             bias=0.0, scale=-1.0), 1)
                        if g >= 1:
                            emit_lnd(g - 1)

                def evac_pdma(tt):
                    def dma_slice(slc):
                        q = 4 * tt + slc
                        t0 = q * 128
                        row = ps2_row(t0)
                        s_pw.inc(s.dma_start(
                            out=ps_scr.ap()[row:row + 128, :],
                            in_=pst[q % 2][:]), q)

                    for slc in range(4):
                        q = 4 * tt + slc
                        s_pw.wait_item(s, q - 2)
                        for hf in range(2):
                            idx = 8 * tt + 2 * slc + hf
                            c_wc.wait(s, idx + 1)
                            c_pev.inc(s.copy(pst[q % 2][:, hf * 512:(hf + 1) * 512],
                                             pwc[idx % 2][:]), 1)
                        if slc >= 1:
                            dma_slice(slc - 1)
                    dma_slice(3)

                def prep_act(qd):
                    c_rcp.wait(s, qd + 1)
                    for j in range(4 * qd, min(4 * qd + 4, NGU)):
                        _w, _scr, cols, kind, row = PREP[j]
                        c_wt1.inc(s.activation(
                            wld[j % 4][:, 0:cols], wld[j % 4][:, 0:cols],
                            AF.Identity, bias=biasM[:],
                            scale=rcp[:, j % 4:j % 4 + 1]), 1)

                stats_act(0)
                if NTT > 1:
                    stats_act(1)
                for tt in range(NTT):
                    if tt + 2 < NTT:
                        stats_act(tt + 2)
                    chain_act(tt)
                    if tt >= 1:
                        evac_pdma(tt - 1)
                    f1_act(s, ACT_MAP.get(tt, []))
                    if tt < NPQ:
                        prep_act(tt)
                emit_lnd(NEC * NTT - 1)
                evac_pdma(NTT - 1)
                for qd in range(NTT, NPQ):
                    prep_act(qd)

            @block.sync
            def _(sy):
                for tt0 in range(min(3, NTT)):
                    src = rows3(xb.ap()[tt0 * 512:(tt0 + 1) * 512, :])
                    s_x.inc(sy.dma_start(out=sb3(x_t[tt0 % 3][:], 4), in_=src), tt0)
                for tt in range(NTT):
                    c_xn.wait(sy, 4 * (tt + 1))
                    s_xnw.inc(sy.dma_start(
                        out=rows3(xn_scr.ap()[tt * 512:(tt + 1) * 512, :]),
                        in_=sb3(xnst[tt % 2][:], 4)), tt)
                    s_xnw.wait_item(sy, tt)
                    c_teT.wait(sy, tt - 2)
                    for dc in range(ND):
                        s_xnr.inc(sy.dma_start(
                            out=xnf[tt % 2][:, dc * 512:(dc + 1) * 512],
                            in_=xn_scr.ap()[tt * 512:(tt + 1) * 512,
                                            dc * 128:(dc + 1) * 128],
                            transpose=True), tt)
                    if tt + 3 < NTT:
                        c_xn.wait(sy, 4 * (tt + 1))
                        c_sq.wait(sy, 4 * (tt + 1))
                        src = rows3(xb.ap()[(tt + 3) * 512:(tt + 4) * 512, :])
                        s_x.inc(sy.dma_start(out=sb3(x_t[tt % 3][:], 4), in_=src),
                                tt + 3)
                    if 1 <= tt and tt - 1 < NPQ:
                        for j in range(4 * (tt - 1), min(4 * tt, NGU)):
                            _w, scr, cols, kind, row = PREP[j]
                            c_wq.wait(sy, j + 1)
                            s_fww.inc(sy.dma_start(
                                out=scr.ap()[row * 128:(row + 1) * 128, :],
                                in_=wqst[j % 4][:, 0:cols]), j)
                for qd in range(min(NTT - 1, NPQ), NPQ):
                    for j in range(4 * qd, min(4 * qd + 4, NGU)):
                        _w, scr, cols, kind, row = PREP[j]
                        c_wq.wait(sy, j + 1)
                        s_fww.inc(sy.dma_start(
                            out=scr.ap()[row * 128:(row + 1) * 128, :],
                            in_=wqst[j % 4][:, 0:cols]), j)

            @block.tensor
            def _(te):
                c_wcast.wait(te, 4)

                def proj(tt):
                    s_xnr.wait_item(te, tt)
                    for ec in range(NEC):
                        g = NEC * tt + ec
                        par = g % 2
                        if g >= 2:
                            c_k.wait(te, g - 1)
                            c_kvf.wait(te, g - 1)
                            c_e.wait(te, g - 1)
                        for wsb, psm, chn in ((wk_sb, pk, c_pk), (wv_sb, pv, c_pv),
                                              (wq_sb, pq, c_pq)):
                            for dc in range(ND):
                                last = te.matmul(
                                    psm[par][:],
                                    wsb[:, dc * EHW + ec * 128: dc * EHW + (ec + 1) * 128],
                                    xnf[tt % 2][:, dc * 512:(dc + 1) * 512],
                                    start=(dc == 0), stop=(dc == ND - 1))
                            chn.inc(last, 1)

                def wc(tt):
                    c_O.wait(te, NEC * (tt + 1))
                    first_wc = True
                    for slc in range(4):
                        for hf in range(2):
                            idx = 8 * tt + 2 * slc + hf
                            if idx >= 2:
                                c_pev.wait(te, idx - 1)
                            for ec in range(NEC):
                                last = te.matmul(
                                    pwc[idx % 2][:],
                                    Obuf[tt % 2][:, ec * 512 + slc * 128:
                                                  ec * 512 + (slc + 1) * 128],
                                    wc_sb[:, ec * D + hf * 512: ec * D + (hf + 1) * 512],
                                    start=(ec == 0), stop=(ec == NEC - 1))
                                if first_wc and ec == 0:
                                    c_teT.inc(last, 1)
                                    first_wc = False
                            c_wc.inc(last, 1)

                for tt in range(NTT):
                    proj(tt)
                    if tt >= 1:
                        wc(tt - 1)
                wc(NTT - 1)

        sFF = contextlib.ExitStack()
        with sFF:
            def sbf(name, shape, dt=F32):
                return sFF.enter_context(nc.sbuf_tensor(name, shape, dt))


            if ACT_POST:
                @block.scalar
                def _(s):
                    f1_act(s, ACT_POST)
            if DVE_POST:
                @block.vector
                def _(v):
                    f1_dve(v, DVE_POST)

            sPD = contextlib.ExitStack()
            with sPD:
                wldd = [sPD.enter_context(nc.sbuf_tensor(f"wldd{i}", [128, DI], F32))
                        for i in range(2)]
                wqsd = [sPD.enter_context(nc.sbuf_tensor(f"wqsd{i}", [128, DI], BF16))
                        for i in range(2)]
                wtmpd = sPD.enter_context(nc.sbuf_tensor("wtmpd", [128, DI], F32))

                @block.gpsimd
                def _(g):
                    for hc in range(NHC):
                        if hc >= 2:
                            c_wq.wait(g, NGU + hc - 1)
                        else:
                            s_pw.wait_upto(g, 4 * NTT - 1)
                            c_pev.wait(g, 8 * NTT)
                            c_wc.wait(g, 8 * NTT)
                        s_fw.inc(g.dma_start(
                            out=wldd[hc % 2][:],
                            in_=wd_d.ap()[hc * 128:(hc + 1) * 128, :]), NGU + hc)

                @block.vector
                def _(v):
                    for p2 in range(NHC // 2):
                        hcs = [2 * p2, 2 * p2 + 1]
                        for hc in hcs:
                            s_fw.wait_item(v, NGU + hc)
                            s_fww.wait_item(v, NGU + hc - 2)
                            v.reduce_max(rmx[:, hc % 2:hc % 2 + 1] if False else sdd[:, hc % 2:hc % 2 + 1],
                                         wldd[hc % 2][:], AX.X,
                                         apply_absolute_value=True)
                        v.drain()
                        for hc in hcs:
                            v.tensor_scalar(sdd[:, 2 + hc % 2:3 + hc % 2],
                                            sdd[:, hc % 2:hc % 2 + 1],
                                            1.0 / 127.0, None, ALU.mult)
                        v.drain()
                        for hc in hcs:
                            v.reciprocal(rcp[:, hc % 2:hc % 2 + 1],
                                         sdd[:, 2 + hc % 2:3 + hc % 2])
                        v.drain()
                        for hc in hcs:
                            v.tensor_scalar(wtmpd[:], wldd[hc % 2][:],
                                            rcp[:, hc % 2:hc % 2 + 1], MAGIC,
                                            ALU.mult, ALU.add)
                            v.tensor_scalar(wtmpd[:], wtmpd[:],
                                            MAGIC, None, ALU.subtract)
                            c_wq.inc(v.tensor_scalar(
                                wqsd[hc % 2][:], wtmpd[:],
                                sdd[:, 2 + hc % 2:3 + hc % 2], None, ALU.mult), 1)

                @block.scalar
                def _(s):
                    for hc in range(NHC):
                        c_wq.wait(s, NGU + hc + 1)
                        s_fww.inc(s.dma_start(
                            out=wd_scr.ap()[hc * 128:(hc + 1) * 128, :],
                            in_=wqsd[hc % 2][:]), NGU + hc)


            sWG = contextlib.ExitStack()
            with sWG:
                wgT = sWG.enter_context(nc.sbuf_tensor("wgT", [128, NHC * DI], BF16))
                wuT = sWG.enter_context(nc.sbuf_tensor("wuT", [128, NHC * DI], BF16))

                @block.sync
                def _(sy):
                    s_pw.wait_upto(sy, 4 * NTT - 1)
                    c_pev.wait(sy, 8 * NTT)
                    c_wc.wait(sy, 8 * NTT)
                    s_fww.wait_upto(sy, NW - 1)
                    for wi, (scr, dst) in enumerate(((wg_scr, wgT), (wu_scr, wuT))):
                        for hc in range(NHC):
                            s_fwr.inc(sy.dma_start(
                                out=dst[:, hc * DI:(hc + 1) * DI],
                                in_=scr.ap()[0:DI, hc * 128:(hc + 1) * 128],
                                transpose=True), 16)

                sF1 = contextlib.ExitStack()
                with sF1:
                    @block.gpsimd
                    def _(g):
                        c_f1.wait(g, 1)
                        s_mx.inc(g.dma_start(out=mx1_i.ap(), in_=mx1[:]), 16)
                        g.wait_ge(s_mx.sem, 16)
                        s_cc.inc(g.collective_compute(
                            "AllReduce", ALU.max, replica_groups=ALL8,
                            ins=[mx1_i.ap().opt()], outs=[mx1_o.ap().opt()]), 1)

                    @block.sync
                    def _(sy):
                        s_cc.wait(sy, NCH + 1)
                        s_gx1.inc(sy.dma_start(out=mxrow[:], in_=mx1_o.ap()), 16)
                        s_sc.wait(sy, 32)
                        s_gx1.inc(sy.dma_start(out=bc_s1[:], in_=bcast_ap(sc_scr, 0)), 16)
                        s_gx1.inc(sy.dma_start(out=bc_is1[:], in_=bcast_ap(sc_scr, 1)), 16)

                    @block.vector
                    def _(v):
                        c_e1.wait(v, 1)
                        s_gx1.wait(v, 16)
                        v.reduce_max(gxt[:], mxrow[0:1, :], AX.X)
                        v.drain()
                        v.tensor_scalar(t11a[:], e1t[:], 1.0, None, ALU.add)
                        v.drain()
                        v.reciprocal(t11b[:], t11a[:])
                        v.drain()
                        v.tensor_scalar(s1t[:], t11b[:], gxt[0:1, 0:1], 1.0 / 127.0,
                                        ALU.mult, ALU.mult)
                        v.drain()
                        c_s1.inc(v.reciprocal(is1t[:], s1t[:]), 1)
                        s_gx1.wait(v, 48)
                        v.tensor_scalar(rs2_arr[:], rstd2_arr[:], bc_is1[:, 0:1],
                                        None, ALU.mult)
                        v.tensor_scalar(sgs1[:], swg_arr[:], bc_s1[:, 0:1],
                                        None, ALU.mult)
                        c_s1.inc(v.tensor_scalar(sus1[:], swu_arr[:], bc_s1[:, 0:1],
                                                 None, ALU.mult), 1)
                        v.drain()

                    @block.scalar
                    def _(s):
                        c_s1.wait(s, 1)
                        s_sc.inc(s.dma_start(out=sc_scr.ap()[0:1, 0:1], in_=s1t[:]), 16)
                        s_sc.inc(s.dma_start(out=sc_scr.ap()[0:1, 1:2], in_=is1t[:]), 16)

                sF2 = contextlib.ExitStack()
                with sF2:
                    xqst = [sF2.enter_context(nc.sbuf_tensor(f"xqst{i}", [128, 4 * D], BF16))
                            for i in range(2)]
                    yq = [sF2.enter_context(nc.sbuf_tensor(f"yq{i}", [128, 4 * D], BF16))
                          for i in range(2)]
                    xqf = [sF2.enter_context(nc.sbuf_tensor(f"xqf{i}", [128, NHC * 512], BF16))
                           for i in range(2)]
                    sgb = [sF2.enter_context(nc.sbuf_tensor(f"sgb{i}", [128, 512], BF16))
                           for i in range(2)]
                    hhst = [sF2.enter_context(nc.sbuf_tensor(f"hhst{i}", [128, 512], BF16))
                            for i in range(2)]
                    xnb2 = sF2.enter_context(nc.sbuf_tensor("xnb2", [128, D], F32))
                    xnb3 = sF2.enter_context(nc.sbuf_tensor("xnb3", [128, D], F32))
                    pg = [sF2.enter_context(nc.psum_tensor(f"pg{i}", [128, 512], F32))
                          for i in range(2)]
                    pu = [sF2.enter_context(nc.psum_tensor(f"pu{i}", [128, 512], F32))
                          for i in range(2)]

                    @block.vector
                    def _(v):
                        def quant(gq):
                            for n in range(4):
                                t = 4 * gq + n
                                if n == 0:
                                    s_xqw.wait_item(v, gq - 2)
                                    s_yq.wait_item(v, gq)
                                ys = yq[gq % 2][:, n * D:(n + 1) * D]
                                v.tensor_scalar(xnb2[:], ys, m_arr[:, t:t + 1],
                                                rs2_arr[:, t:t + 1],
                                                ALU.subtract, ALU.mult)
                                v.tensor_scalar(xnb3[:], xnb2[:], MAGIC, MAGIC,
                                                ALU.add, ALU.subtract)
                                c_xq.inc(v.tensor_scalar(
                                    xqst[gq % 2][:, n * D:(n + 1) * D], xnb3[:],
                                    127.0, -128.0, ALU.min, ALU.max), 1)

                        def hhwork(gq):
                            for ic in range(NIC):
                                gi = NIC * gq + ic
                                c_sg.wait(v, gi + 1)
                                c_gu.wait(v, gi + 1)
                                s_hw.wait_item(v, gi - 2)
                                c_hh.inc(v.scalar_tensor_tensor(
                                    hhst[gi % 2][:], pu[gi % 2][:],
                                    sus1[:, ic:ic + 1], sgb[gi % 2][:],
                                    ALU.mult, ALU.mult), 1)
                                v.reduce_max(mxb22[:, ic:ic + 1], hhst[gi % 2][:],
                                             AX.X, apply_absolute_value=True)
                            v.drain()
                            v.reduce_max(mxbb[:], mxb22[:], AX.X)
                            v.drain()
                            ins = v.tensor_max(mx2[:], mx2[:], mxbb[:])
                            if gq == NG - 1:
                                c_mx2.inc(ins, 1)

                        quant(0)
                        for gq in range(NG):
                            if gq + 1 < NG:
                                quant(gq + 1)
                            hhwork(gq)

                    @block.sync
                    def _(sy):
                        for gq in range(min(2, NG)):
                            s_yw.wait_upto(sy, 4 * (gq + 1) - 1)
                            s_yq.inc(sy.dma_start(
                                out=sb3(yq[gq % 2][:], 4),
                                in_=rows3(y_scr.ap()[gq * 512:(gq + 1) * 512, :])),
                                gq)
                        for gq in range(NG):
                            c_xq.wait(sy, 4 * (gq + 1))
                            if gq + 2 < NG:
                                s_yw.wait_upto(sy, 4 * (gq + 3) - 1)
                                s_yq.inc(sy.dma_start(
                                    out=sb3(yq[gq % 2][:], 4),
                                    in_=rows3(y_scr.ap()[(gq + 2) * 512:
                                                         (gq + 3) * 512, :])),
                                    gq + 2)
                            s_xqw.inc(sy.dma_start(
                                out=rows3(xq_scr.ap()[gq * 512:(gq + 1) * 512, :]),
                                in_=sb3(xqst[gq % 2][:], 4)), gq)
                            s_xqw.wait_item(sy, gq)
                            if gq >= 2:
                                c_gu.wait(sy, NIC * (gq - 1))
                            for hc in range(NHC):
                                s_xqr.inc(sy.dma_start(
                                    out=xqf[gq % 2][:, hc * 512:(hc + 1) * 512],
                                    in_=xq_scr.ap()[gq * 512:(gq + 1) * 512,
                                                    hc * 128:(hc + 1) * 128],
                                    transpose=True), gq)

                    @block.tensor
                    def _(te):
                        s_fwr.wait(te, 16 * 2 * NHC)
                        c_pev.wait(te, 8 * NTT)
                        c_k.wait(te, NEC * NTT)
                        c_e.wait(te, NEC * NTT)
                        c_kvf.wait(te, NEC * NTT)
                        for gq in range(NG):
                            s_xqr.wait_item(te, gq)
                            for ic in range(NIC):
                                gi = NIC * gq + ic
                                par = gi % 2
                                if gi >= 2:
                                    c_sg.wait(te, gi - 1)
                                    c_hh.wait(te, gi - 1)
                                for wT, psm, chn in ((wgT, pg, c_pg), (wuT, pu, c_gu)):
                                    for hc in range(NHC):
                                        last = te.matmul(
                                            psm[par][:],
                                            wT[:, hc * DI + ic * 128: hc * DI + (ic + 1) * 128],
                                            xqf[gq % 2][:, hc * 512:(hc + 1) * 512],
                                            start=(hc == 0), stop=(hc == NHC - 1))
                                    chn.inc(last, 1)

                    @block.scalar
                    def _(s):
                        c_s1.wait(s, 2)
                        for gq in range(NG):
                            for ic in range(NIC):
                                gi = NIC * gq + ic
                                c_pg.wait(s, gi + 1)
                                if gi >= 2:
                                    c_hh.wait(s, gi - 1)
                                c_sg.inc(s.activation(sgb[gi % 2][:], pg[gi % 2][:],
                                                      AF.Silu, bias=0.0,
                                                      scale=sgs1[:, ic:ic + 1]), 1)
                                if gi >= 1:
                                    pj = gi - 1
                                    c_hh.wait(s, pj + 1)
                                    s_hw.inc(s.dma_start(
                                        out=hh_scr.ap()[(pj % NIC) * 128:(pj % NIC + 1) * 128,
                                                        (pj // NIC) * 512:(pj // NIC + 1) * 512],
                                        in_=hhst[pj % 2][:]), pj)
                        pj = NIC * NG - 1
                        c_hh.wait(s, pj + 1)
                        s_hw.inc(s.dma_start(
                            out=hh_scr.ap()[(pj % NIC) * 128:(pj % NIC + 1) * 128,
                                            (pj // NIC) * 512:(pj // NIC + 1) * 512],
                            in_=hhst[pj % 2][:]), pj)

                    @block.gpsimd
                    def _(g):
                        c_mx2.wait(g, 1)
                        s_mx.inc(g.dma_start(out=mx2_i.ap(), in_=mx2[:]), 16)
                        g.wait_ge(s_mx.sem, 32)
                        s_cc.inc(g.collective_compute(
                            "AllReduce", ALU.max, replica_groups=ALL8,
                            ins=[mx2_i.ap().opt()], outs=[mx2_o.ap().opt()]), 1)

                    @block.vector
                    def _(v):
                        c_e1.wait(v, 2)
                        s_gx2.wait(v, 16)
                        v.reduce_max(gx2t[:], mxrow2[0:1, :], AX.X)
                        v.drain()
                        v.tensor_scalar(t11a[:], e1t2[:], 1.0, None, ALU.add)
                        v.drain()
                        v.reciprocal(t11b[:], t11a[:])
                        v.drain()
                        v.tensor_scalar(s2t[:], t11b[:], gx2t[0:1, 0:1], 1.0 / 127.0,
                                        ALU.mult, ALU.mult)
                        v.drain()
                        c_s2.inc(v.reciprocal(is2t[:], s2t[:]), 1)

                    @block.sync
                    def _(sy):
                        s_cc.wait(sy, NCH + 2)
                        s_gx2.inc(sy.dma_start(out=mxrow2[:], in_=mx2_o.ap()), 16)

                    @block.scalar
                    def _(s):
                        c_s2.wait(s, 1)
                        s_sc.inc(s.dma_start(out=sc_scr.ap()[0:1, 2:3], in_=s2t[:]), 16)
                        s_sc.inc(s.dma_start(out=sc_scr.ap()[0:1, 3:4], in_=is2t[:]), 16)

            sF3 = contextlib.ExitStack()
            with sF3:
                wdT = sF3.enter_context(nc.sbuf_tensor("wdT", [128, NIC * D], BF16))
                hhld = [sF3.enter_context(nc.sbuf_tensor(f"hhld{i}", [128, NIC * 512], BF16))
                        for i in range(2)]
                hbuf = [sF3.enter_context(nc.sbuf_tensor(f"hbuf{i}", [128, NIC * 512], BF16))
                        for i in range(2)]
                tqa = sF3.enter_context(nc.sbuf_tensor("tqa", [128, 512], F32))
                outst = [sF3.enter_context(nc.sbuf_tensor(f"outst{i}", [128, 512], F32))
                         for i in range(2)]
                yo = [sF3.enter_context(nc.sbuf_tensor(f"yo{i}", [128, 4 * D], BF16))
                      for i in range(2)]
                pd = [sF3.enter_context(nc.psum_tensor(f"pd{i}", [128, 512], F32))
                      for i in range(8)]

                @block.scalar
                def _(s):
                    c_gu.wait(s, NIC * NG)
                    c_hh.wait(s, NIC * NG)
                    s_hw.wait_upto(s, NIC * NG - 1)
                    s_xqw.wait_upto(s, NG - 1)
                    s_fww.wait_upto(s, NW - 1)
                    for ic in range(NIC):
                        s_wdr.inc(s.dma_start(
                            out=wdT[:, ic * D:(ic + 1) * D],
                            in_=wd_scr.ap()[0:D, ic * 128:(ic + 1) * 128],
                            transpose=True), 16)

                @block.sync
                def _(sy):
                    c_gu.wait(sy, NIC * NG)
                    c_hh.wait(sy, NIC * NG)
                    s_hw.wait_upto(sy, NIC * NG - 1)

                    def hld(gq):
                        s_hw.wait_upto(sy, NIC * (gq + 1) - 1)
                        if gq >= 2:
                            c_h.wait(sy, NIC * (gq - 1))
                            c_oev.wait(sy, 8 * (gq - 1))
                        for ic in range(NIC):
                            s_hr.inc(sy.dma_start(
                                out=hhld[gq % 2][:, ic * 512:(ic + 1) * 512],
                                in_=hh_scr.ap()[ic * 128:(ic + 1) * 128,
                                                gq * 512:(gq + 1) * 512]), gq)
                        s_yo.inc(sy.dma_start(
                            out=sb3(yo[gq % 2][:], 4),
                            in_=rows3(y_scr.ap()[gq * 512:(gq + 1) * 512, :])), gq)

                    for gq in range(min(2, NG)):
                        hld(gq)
                    s_sc.wait(sy, 64)
                    s_gx2.inc(sy.dma_start(out=bc_s2[:], in_=bcast_ap(sc_scr, 2)), 16)
                    s_gx2.inc(sy.dma_start(out=bc_is2[:], in_=bcast_ap(sc_scr, 3)), 16)
                    for gq in range(2, NG):
                        hld(gq)
                    s_out.wait_upto(sy, 2 * NOB - 1)

                @block.vector
                def _(v):
                    s_gx2.wait(v, 48)

                    def quant(gq):
                        for ic in range(NIC):
                            gi = NIC * gq + ic
                            if ic == 0:
                                s_hr.wait_item(v, gq)
                                if gq >= 2:
                                    c_dn.wait(v, 8 * (gq - 1))
                            v.tensor_scalar(tqa[:], hhld[gq % 2][:, ic * 512:(ic + 1) * 512],
                                            bc_is2[:, 0:1], MAGIC, ALU.mult, ALU.add)
                            v.tensor_scalar(tqa[:], tqa[:], MAGIC, 127.0,
                                            ALU.subtract, ALU.min)
                            c_h.inc(v.tensor_scalar(
                                hbuf[gq % 2][:, ic * 512:(ic + 1) * 512], tqa[:],
                                -128.0, None, ALU.max), 1)

                    def outevac(gq, h):
                        if h == 0:
                            s_yo.wait_item(v, gq)
                        for n in (2 * h, 2 * h + 1):
                            t = 4 * gq + n
                            for hf in range(2):
                                idx = 2 * t + hf
                                s_out.wait_item(v, idx - 2)
                                c_dn.wait(v, idx + 1)
                                c_oev.inc(v.scalar_tensor_tensor(
                                    outst[idx % 2][:],
                                    pd[idx % 8][:], bc_s2[:, 0:1],
                                    yo[gq % 2][:, n * D + hf * 512:
                                               n * D + (hf + 1) * 512],
                                    ALU.mult, ALU.add), 1)

                    quant(0)
                    for gq in range(NG):
                        if gq + 1 < NG:
                            outevac(gq, 0)
                            quant(gq + 1)
                            outevac(gq, 1)
                        else:
                            outevac(gq, 0)
                            outevac(gq, 1)

                @block.tensor
                def _(te):
                    s_wdr.wait(te, 16 * NIC)
                    c_sg.wait(te, NIC * NG)
                    c_hh.wait(te, NIC * NG)
                    for gq in range(NG):
                        for h in range(2):
                            if gq >= 1:
                                c_oev.wait(te, 8 * (gq - 1) + 4 * (h + 1))
                            for ic in range(NIC):
                                if h == 0:
                                    c_h.wait(te, NIC * gq + ic + 1)
                                for n in (2 * h, 2 * h + 1):
                                    t = 4 * gq + n
                                    for hf in range(2):
                                        idx = 2 * t + hf
                                        last = te.matmul(
                                            pd[idx % 8][:],
                                            hbuf[gq % 2][:, ic * 512 + n * 128:
                                                          ic * 512 + (n + 1) * 128],
                                            wdT[:, ic * D + hf * 512: ic * D + (hf + 1) * 512],
                                            start=(ic == 0), stop=(ic == NIC - 1))
                                        if ic == NIC - 1:
                                            c_dn.inc(last, 1)

                @block.scalar
                def _(s):
                    for t in range(NOB):
                        for hf in range(2):
                            idx = 2 * t + hf
                            c_oev.wait(s, idx + 1)
                            s_out.inc(s.dma_start(
                                out=out_d.ap()[t * 128:(t + 1) * 128,
                                               hf * 512:(hf + 1) * 512],
                                in_=outst[idx % 2][:]), idx)

    return nc


_NC_CACHE = {}


def _get_nc(LQ, D, DI):
    key = (LQ, D, DI)
    if key not in _NC_CACHE:
        _NC_CACHE[key] = build_nc(LQ, D, DI)
    return _NC_CACHE[key]


def kernel(L0_attn_X, L0_attn_Wq, L0_attn_Wk, L0_attn_Wv, L0_attn_Wc,
           L0_ffn_clip_x, L0_ffn_clip_down, L0_ffn_w_gate, L0_ffn_w_up,
           L0_ffn_w_down):
    X = np.ascontiguousarray(np.asarray(L0_attn_X, dtype=np.float32))
    Wq = np.ascontiguousarray(np.asarray(L0_attn_Wq, dtype=np.float32))
    Wk = np.ascontiguousarray(np.asarray(L0_attn_Wk, dtype=np.float32))
    Wv = np.ascontiguousarray(np.asarray(L0_attn_Wv, dtype=np.float32))
    Wc = np.ascontiguousarray(np.asarray(L0_attn_Wc, dtype=np.float32))
    Wg = np.ascontiguousarray(np.asarray(L0_ffn_w_gate, dtype=np.float32))
    Wu = np.ascontiguousarray(np.asarray(L0_ffn_w_up, dtype=np.float32))
    Wd = np.ascontiguousarray(np.asarray(L0_ffn_w_down, dtype=np.float32))
    cx = np.asarray(L0_ffn_clip_x, dtype=np.float32).reshape(1, 1)
    cd = np.asarray(L0_ffn_clip_down, dtype=np.float32).reshape(1, 1)

    B, LQ, D = X.shape
    DI = Wg.shape[0]
    EHW = D // 2
    q4 = LQ // 4
    SPANS = [(0, q4), (q4, 2 * q4), (2 * q4, 3 * q4),
             (3 * q4, 3 * q4 + q4 // 2), (3 * q4 + q4 // 2, 4 * q4)]

    def own_rows(eh):
        return np.concatenate(
            [np.arange(s + eh * (e - s) // 2, s + (eh + 1) * (e - s) // 2)
             for s, e in SPANS])

    nc = _get_nc(LQ, D, DI)
    in_maps = []
    for c in range(8):
        b, eh = c >> 1, c & 1
        in_maps.append(dict(
            xb=X[b],
            x_own=np.ascontiguousarray(X[b][own_rows(eh)]),
            wq=np.ascontiguousarray(Wq[:, eh * EHW:(eh + 1) * EHW]),
            wk=np.ascontiguousarray(Wk[:, eh * EHW:(eh + 1) * EHW]),
            wv=np.ascontiguousarray(Wv[:, eh * EHW:(eh + 1) * EHW]),
            wc=np.ascontiguousarray(Wc[eh * EHW:(eh + 1) * EHW, :]),
            wg=Wg, wu=Wu, wd=Wd, clipx=cx, clipd=cd,
        ))
    res = run_bass_kernel_spmd(nc, in_maps, core_ids=list(range(8)))
    out = np.empty((B, LQ, D), dtype=np.float32)
    for c in range(8):
        b, eh = c >> 1, c & 1
        out[b, own_rows(eh)] = np.asarray(res.results[c]["out"])
    return out
